# version 03
# baseline (speedup 1.0000x reference)
import numpy as np
import ml_dtypes
import concourse.bass as bass
import concourse.mybir as mybir
from concourse.bass_utils import run_bass_kernel_spmd
from contextlib import ExitStack

F32 = mybir.dt.float32
BF16 = mybir.dt.bfloat16
AF = mybir.ActivationFunctionType
ALU = mybir.AluOpType
AX = mybir.AxisListType
BF = ml_dtypes.bfloat16

D = 4096
S = 2048
B = 4
NH = 32
TOWN = 1024
TALL = 2048
NCH = 32
SCALE = float(1.0 / np.sqrt(128.0))


class Prog:

    def __init__(self):
        self.q = {"sync": [], "vector": [], "scalar": [], "tensor": [], "gpsimd": []}
        self.cnt = {}
        self.note = {}

    def bump(self, sem, delta=1):
        self.cnt[sem] = self.cnt.get(sem, 0) + delta
        return self.cnt[sem]

    def val(self, sem):
        return self.cnt.get(sem, 0)

    def emit(self, eng, fn):
        self.q[eng].append(fn)


def build(taps=()):
    nc = bass.Bass("TRN2", target_bir_lowering=False, debug=False)

    x_loc = nc.dram_tensor("x_loc", [TALL, D], F32, kind="ExternalInput").ap()
    xa = nc.dram_tensor("xa", [3072, D], F32, kind="ExternalInput").ap()
    wq_pk = nc.dram_tensor("wq_pk", [96, 8, 128, 512], F32, kind="ExternalInput").ap()
    b_qkv = nc.dram_tensor("b_qkv", [3 * D], F32, kind="ExternalInput").ap()
    wo_pk = nc.dram_tensor("wo_pk", [32, 8, 128, 512], F32, kind="ExternalInput").ap()
    b_out = nc.dram_tensor("b_out", [D], F32, kind="ExternalInput").ap()
    wf_pk = nc.dram_tensor("wf_pk", [32, 8, 128, 512], F32, kind="ExternalInput").ap()
    W_T8 = nc.dram_tensor("W_T8", [D, 8], F32, kind="ExternalInput").ap()
    b_ffn = nc.dram_tensor("b_ffn", [D], F32, kind="ExternalInput").ap()
    A_in = nc.dram_tensor("A_in", [D, 32], F32, kind="ExternalInput").ap()
    Bmat = nc.dram_tensor("Bmat", [32, D], F32, kind="ExternalInput").ap()
    LiT = nc.dram_tensor("LiT", [32, 8], F32, kind="ExternalInput").ap()
    Li = nc.dram_tensor("Li", [8, 32], F32, kind="ExternalInput").ap()
    alibi_cols = nc.dram_tensor("alibi_cols", [128, 512], F32, kind="ExternalInput").ap()
    ctab = nc.dram_tensor("ctab", [32, TOWN], F32, kind="ExternalInput").ap()
    pm_tile = nc.dram_tensor("pm_tile", [128, 128], F32, kind="ExternalInput").ap()

    out_loc = nc.dram_tensor("out_loc", [TOWN, D], F32, kind="ExternalOutput").ap()

    ident_b = nc.inline_tensor(np.eye(128, dtype=BF), name="ident_b").ap()
    ident_f = nc.inline_tensor(np.eye(128, dtype=np.float32), name="ident_f").ap()
    ones_bi = nc.inline_tensor(np.ones((128, 1), dtype=BF), name="ones_bi").ap()
    tri_np = np.where(np.arange(128)[:, None] <= np.arange(128)[None, :], 0.0, -1e9).astype(np.float32)
    tri_in = nc.inline_tensor(tri_np, name="tri_in").ap()

    kv_fm = nc.dram_tensor("kv_fm", [2 * D, TALL], BF16, kind="Internal").ap()
    q_fm = nc.dram_tensor("q_fm", [D, TOWN], BF16, kind="Internal").ap()
    sc_bounce = nc.dram_tensor("sc_bounce", [1, 2], F32, kind="Internal").ap()
    el_bounce = nc.dram_tensor("el_bounce", [1, TOWN], F32, kind="Internal").ap()

    tapsh = {}
    tapspec = {"xn_own": ([128, NCH * TOWN // 2], "bf"), "xn_part": ([128, NCH * TOWN // 2], "bf"),
               "kv_dump": ([2 * D, TALL // 2], "bf"), "q_dump": ([D, TOWN // 2], "bf"),
               "attn_out": ([128, NH * TOWN // 2], "bf"), "scales": ([1, 16], "f"),
               "wfull": ([8, 512], "f"), "z_own": ([32, TOWN], "f"),
               "zp": ([32, TOWN // 2], "bf")}
    tap_is_bf = {}
    for t in taps:
        sh, dt = tapspec[t]
        tap_is_bf[t] = dt == "bf"
        tapsh[t] = nc.dram_tensor("tap_" + t, sh, F32, kind="ExternalOutput").ap()

    P = Prog()

    with ExitStack() as stx:
        e = stx.enter_context

        xn_own = e(nc.sbuf_tensor("xn_own", [128, NCH * TOWN], BF16))
        xn_part = e(nc.sbuf_tensor("xn_part", [128, NCH * TOWN], BF16))
        xstage = [e(nc.sbuf_tensor("xstage0", [128, D], F32))]
        xn_tm = e(nc.sbuf_tensor("xn_tm", [128, D], BF16))
        wgst = [e(nc.sbuf_tensor(f"wgst{i}", [128, 512], F32)) for i in range(2)]
        wgbf = [e(nc.sbuf_tensor(f"wgbf{i}", [128, 512], BF16)) for i in range(2)]
        kvev = [e(nc.sbuf_tensor(f"kvev{i}", [128, TALL], BF16)) for i in range(2)]
        bqc = e(nc.sbuf_tensor("bqc", [128, 96], F32))
        bfc = e(nc.sbuf_tensor("bfc", [128, 32], F32))
        boc = e(nc.sbuf_tensor("boc", [128, 32], F32))
        b8bc = e(nc.sbuf_tensor("b8bc", [128, 8], F32))
        btmp = e(nc.sbuf_tensor("btmp", [96, 128], F32))
        A_bf = e(nc.sbuf_tensor("A_bf", [128, NCH * 32], BF16))
        bm_st = [e(nc.sbuf_tensor(f"bm_st{i}", [32, 128], F32)) for i in range(2)]
        bm_bf = [e(nc.sbuf_tensor(f"bm_bf{i}", [32, 128], BF16)) for i in range(2)]
        Lis_bf = e(nc.sbuf_tensor("Lis_bf", [8, 32], BF16))
        LiT_bf = e(nc.sbuf_tensor("LiT_bf", [32, 8], BF16))
        w8_bf = e(nc.sbuf_tensor("w8_bf", [128, NCH * 8], BF16))
        ib_sb = e(nc.sbuf_tensor("ib_sb", [128, 128], BF16))
        if_sb = e(nc.sbuf_tensor("if_sb", [128, 128], F32))
        ones_sb = e(nc.sbuf_tensor("ones_sb", [128, 1], BF16))
        tri_sb = e(nc.sbuf_tensor("tri_sb", [128, 128], F32))
        pm_sb = e(nc.sbuf_tensor("pm_sb", [128, 128], F32))
        alibi_sb = e(nc.sbuf_tensor("alibi_sb", [128, 512], F32))
        kh = xn_tm[:, 0:TALL]
        vh = xn_tm[:, TALL:2 * TALL]
        vt = vh
        qh = e(nc.sbuf_tensor("qh", [128, TOWN], BF16))
        pt_sb = [e(nc.sbuf_tensor(f"pt{i}", [128, TOWN], BF16)) for i in range(2)]
        zfm_bf = e(nc.sbuf_tensor("zfm_bf", [32, 512], BF16))
        linv_bc = e(nc.sbuf_tensor("linv_bc", [128, TOWN], F32))
        ellinv = linv_bc
        att_fm = xn_part
        z_own = e(nc.sbuf_tensor("z_own", [32, TOWN], F32))
        zp_bf = e(nc.sbuf_tensor("zp_bf", [32, TOWN], BF16))
        ga = e(nc.sbuf_tensor("ga", [128, 80], F32))
        gw = e(nc.sbuf_tensor("gw", [128, 32], F32))
        wf_fm = e(nc.sbuf_tensor("wf_fm", [8, 512], F32))
        wf_bf = e(nc.sbuf_tensor("wf_bf", [8, TOWN], BF16))
        sc_acc = e(nc.sbuf_tensor("sc_acc", [8, 4], F32))
        sc_fin = e(nc.sbuf_tensor("sc_fin", [1, 16], F32))
        sc_bc = e(nc.sbuf_tensor("sc_bc", [128, 2], F32))
        hscr = wf_fm
        g_sb = kvev[0][:].bitcast(F32)
        s_sb = kvev[1][:].bitcast(F32)
        st_scr = kvev[0][:].bitcast(F32)
        cb_sb = kvev[1][:].bitcast(F32)
        xr_sb = linv_bc
        o_sb = e(nc.sbuf_tensor("o_sb", [128, TOWN], F32))

        ps = [e(nc.psum_tensor(f"ps{i}", [128, 512], F32)) for i in range(8)]

        s_in = e(nc.semaphore("s_in"))
        s_wr = e(nc.semaphore("s_wr"))
        s_w = [e(nc.semaphore(f"s_w{i}")) for i in range(4)]
        s_w2 = [e(nc.semaphore(f"s_w2{i}")) for i in range(4)]
        s_wk = [e(nc.semaphore(f"s_wk{i}")) for i in range(2)]
        s_bm = [e(nc.semaphore(f"s_bm{i}")) for i in range(2)]
        s_cv = e(nc.semaphore("s_cv"))
        s_cp = e(nc.semaphore("s_cp"))
        s_pe = e(nc.semaphore("s_pe"))
        s_ac = e(nc.semaphore("s_ac"))
        s_st = e(nc.semaphore("s_st"))
        SEM = {"s_in": s_in, "s_wr": s_wr, "s_cv": s_cv, "s_cp": s_cp,
               "s_pe": s_pe, "s_ac": s_ac, "s_st": s_st}
        for i in range(4):
            SEM[f"s_w{i}"] = s_w[i]
            SEM[f"s_w2{i}"] = s_w2[i]
        for i in range(2):
            SEM[f"s_wk{i}"] = s_wk[i]
            SEM[f"s_bm{i}"] = s_bm[i]

        def W(eng, sem, v):
            if v > 0:
                P.emit(eng, lambda E, S_=SEM[sem], v=v: E.wait_ge(S_, v))

        def dma(eng, dst, src, sem="s_in"):
            v = P.bump(sem, 16)
            P.emit(eng, lambda E, d=dst, s=src, S_=SEM[sem]: E.dma_start(out=d, in_=s).then_inc(S_, 16))
            return v

        def MM(out, lhsT, rhs, start, stop, inc=False):
            if inc:
                v = P.bump("s_pe")
                P.emit("tensor", lambda T, o=out, l=lhsT, r=rhs, a=start, b=stop: T.matmul(o, l, r, start=a, stop=b).then_inc(s_pe, 1))
                return v
            P.emit("tensor", lambda T, o=out, l=lhsT, r=rhs, a=start, b=stop: T.matmul(o, l, r, start=a, stop=b))
            return None

        def VOP(opname, *args, inc=None, **kw):
            if inc:
                v = P.bump("s_cv")
                P.emit("vector", lambda V, o=opname, args=args, kw=kw: getattr(V, o)(*args, **kw).then_inc(s_cv, 1))
                return v
            P.emit("vector", lambda V, o=opname, args=args, kw=kw: getattr(V, o)(*args, **kw))
            return None

        def VC(opname, *args, **kw):
            v = VOP(opname, *args, inc=True, **kw)
            W("vector", "s_cv", v)
            return v

        def SOP(opname, *args, inc=None, **kw):
            if inc:
                v = P.bump("s_ac")
                P.emit("scalar", lambda Sc, o=opname, args=args, kw=kw: getattr(Sc, o)(*args, **kw).then_inc(s_ac, 1))
                return v
            P.emit("scalar", lambda Sc, o=opname, args=args, kw=kw: getattr(Sc, o)(*args, **kw))
            return None

        dma("sync", ib_sb[:], ident_b)
        dma("sync", if_sb[:], ident_f)
        dma("sync", ones_sb[:], ones_bi)
        dma("sync", alibi_sb[:], alibi_cols)
        dma("sync", tri_sb[:], tri_in)
        dma("sync", pm_sb[:], pm_tile)
        dma("sync", xstage[0][:, 0:1024].rearrange("p (c r) -> p c r", c=32), A_in.rearrange("(c p) r -> p c r", p=128))
        dma("sync", xstage[0][0:8, 1024:1056], Li)
        dma("sync", xstage[0][0:32, 1056:1064], LiT)
        dma("sync", xstage[0][:, 1088:1344].rearrange("p (c e) -> p c e", c=32), W_T8.rearrange("(c p) e -> p c e", p=128))
        dma("gpsimd", b8bc[:], b_ffn[0:8].rearrange("(o e) -> o e", o=1).to_broadcast((128, 8)), sem="s_wr")
        v_smalls = P.val("s_in")
        W("vector", "s_in", v_smalls)
        VOP("tensor_copy", A_bf[:], xstage[0][:, 0:1024])
        VOP("tensor_copy", LiT_bf[:], xstage[0][0:32, 1056:1064])
        VOP("tensor_copy", w8_bf[:], xstage[0][:, 1088:1344])
        v_csm = VOP("tensor_scalar_mul", Lis_bf[:], xstage[0][0:8, 1024:1056], 0.5, inc=True)

        v_bq = dma("sync", btmp[0:96, :], b_qkv.rearrange("(c p) -> c p", p=128))
        W("tensor", "s_in", v_bq)
        v_t1 = MM(ps[0][:, 0:96], btmp[0:96, :], if_sb[0:96, 0:96], True, True, inc=True)
        W("vector", "s_pe", v_t1)
        v_c1 = VOP("tensor_copy", bqc[:], ps[0][:, 0:96], inc=True)
        W("sync", "s_cv", v_c1)
        v_bf2 = dma("sync", btmp[0:32, :], b_ffn.rearrange("(c p) -> c p", p=128))
        W("tensor", "s_in", v_bf2)
        W("tensor", "s_cv", v_c1)
        v_t2 = MM(ps[0][:, 0:32], btmp[0:32, :], if_sb[0:32, 0:32], True, True, inc=True)
        W("vector", "s_pe", v_t2)
        v_c2 = VOP("tensor_copy", bfc[:], ps[0][:, 0:32], inc=True)
        W("sync", "s_cv", v_c2)
        v_bo = dma("sync", btmp[0:32, :], b_out.rearrange("(c p) -> c p", p=128))
        W("tensor", "s_in", v_bo)
        W("tensor", "s_cv", v_c2)
        v_t3 = MM(ps[0][:, 0:32], btmp[0:32, :], if_sb[0:32, 0:32], True, True, inc=True)
        W("vector", "s_pe", v_t3)
        v_c3 = VOP("tensor_copy", boc[:], ps[0][:, 0:32], inc=True)

        def ln_tile(buf, src_rows_ap, tag, dst, dcol, tp_banks):
            if P.note.get("xstage_war") is not None:
                W("sync", "s_ac", P.note["xstage_war"])
            v_x = dma("sync", buf[:], src_rows_ap)
            W("vector", "s_in", v_x)
            v_sum = VOP("reduce_sum", ga[:, 0:1], buf[:], axis=AX.X, inc=True)
            W("scalar", "s_in", v_x)
            pe_war = P.note.get("xn_tm_pewar")
            if pe_war:
                W("scalar", "s_pe", pe_war)
            for pc in range(8):
                SOP("activation", ps[5][:, 0:512], buf[:, pc * 512:(pc + 1) * 512], AF.Square,
                    accum_out=ga[:, 64 + pc:65 + pc], inc=(pc == 7))
            v_sq = P.val("s_ac")
            W("vector", "s_ac", v_sq)
            VC("reduce_sum", ga[:, 1:2], ga[:, 64:72], axis=AX.X)
            VC("tensor_scalar_mul", ga[:, 2:3], ga[:, 0:1], 1.0 / D)
            VC("tensor_scalar_mul", ga[:, 3:4], ga[:, 1:2], 1.0 / D)
            VC("tensor_mul", ga[:, 4:5], ga[:, 2:3], ga[:, 2:3])
            VC("tensor_sub", ga[:, 5:6], ga[:, 3:4], ga[:, 4:5])
            v_var = VC("tensor_scalar_add", ga[:, 5:6], ga[:, 5:6], 1e-5)
            W("scalar", "s_cv", v_var)
            v_rt = SOP("activation", ga[:, 6:7], ga[:, 5:6], AF.Sqrt, inc=True)
            W("vector", "s_ac", v_rt)
            VC("reciprocal", ga[:, 7:8], ga[:, 6:7])
            VC("tensor_mul", ga[:, 8:9], ga[:, 2:3], ga[:, 7:8])
            v_nmr = VC("tensor_scalar_mul", ga[:, 9:10], ga[:, 8:9], -1.0)
            W("scalar", "s_cv", v_nmr)
            v_norm = SOP("activation", xn_tm[:], buf[:], AF.Identity, bias=ga[:, 9:10], scale=ga[:, 7:8], inc=True)
            P.note["xstage_war"] = v_norm
            last_tp = None
            for grp in range(8):
                bank = tp_banks[grp % 2]
                W("tensor", "s_ac", v_norm)
                war = P.note.get("tpev_" + str(tp_banks[0].name if hasattr(tp_banks[0], 'name') else 0) + "_" + str(grp % 2))
                if war:
                    W("tensor", "s_cv", war)
                for k4 in range(4):
                    ch = grp * 4 + k4
                    vv = MM(bank[:, k4 * 128:(k4 + 1) * 128], xn_tm[:, ch * 128:(ch + 1) * 128], ib_sb[:], True, True, inc=(k4 == 3))
                last_tp = vv
                W("vector", "s_pe", vv)

                def _ev(V, bank=bank, grp=grp, dst=dst, dcol=dcol):
                    for k4 in range(4):
                        ch = grp * 4 + k4
                        ins = V.tensor_copy(dst[:, ch * TOWN + dcol: ch * TOWN + dcol + 128], bank[:, k4 * 128:(k4 + 1) * 128])
                        if k4 == 3:
                            ins.then_inc(s_cv, 1)
                vev = P.bump("s_cv")
                P.emit("vector", _ev)
                P.note["tpev_" + str(tp_banks[0].name if hasattr(tp_banks[0], 'name') else 0) + "_" + str(grp % 2)] = vev
            P.note["xn_tm_pewar"] = last_tp
            return v_norm

        W("sync", "s_cv", v_csm)
        for ti in range(16):
            dst = xn_own if ti < 8 else xn_part
            ln_tile(xstage[0], x_loc[ti * 128:(ti + 1) * 128, :], "ln", 
                    dst, (ti % 8) * 128, (ps[6], ps[7]))
        v_xn_cv = P.val("s_cv")

        if "xn_own" in tapsh:
            W("gpsimd", "s_cv", v_xn_cv)
            dma("gpsimd", tapsh["xn_own"], xn_own[:].bitcast(F32), sem="s_st")
            dma("gpsimd", tapsh["xn_part"], xn_part[:].bitcast(F32), sem="s_st")

        slabs = [("KV", o) for o in range(D, 3 * D, 128)] + [("Q", o) for o in range(0, D, 128)]
        gch = 0
        cast_hist = {}
        mm_hist = {}
        evac_par = {0: 0, 1: 0}
        kv_wr = {}

        def wgroup_dma_cast(src_ap):
            g = P.note.get("wg", 0)
            P.note["wg"] = g + 1
            wi = g % 2
            if g >= 2:
                sems, vv = cast_hist[g - 2]
                W("sync", sems, vv)
            v_w = dma("sync", wgst[wi][:], src_ap, sem=f"s_w{wi}")
            ceng = "vector" if (g % 5) != 4 else "gpsimd"
            csem = "s_cv" if ceng == "vector" else "s_cp"
            W(ceng, f"s_w{wi}", v_w)
            if g >= 2:
                W(ceng, "s_pe", mm_hist[g - 2])
            v_c = P.bump(csem)
            cast_hist[g] = (csem, v_c)
            P.emit(ceng, lambda E, a=wgbf[wi], b=wgst[wi], S_=SEM[csem]: E.tensor_copy(a[:], b[:]).then_inc(S_, 1))
            return g, wi, csem, v_c

        for si, (kind, o0) in enumerate(slabs):
            ntc = 4 if kind == "KV" else 2
            banks = [ps[(si % 2) * 4 + t] for t in range(ntc)]
            for grp in range(8):
                g, wi, csem, v_c = wgroup_dma_cast(wq_pk[o0 // 128, grp])
                W("tensor", csem, v_c)
                if si == 0 and grp == 0:
                    W("tensor", "s_cv", v_xn_cv)
                if grp == 0:
                    W("tensor", "s_ac", evac_par[si % 2])
                for c4 in range(4):
                    dch = grp * 4 + c4
                    first = dch == 0
                    lhsT = wgbf[wi][:, c4 * 128:(c4 + 1) * 128]
                    for tc in range(ntc):
                        if kind == "Q":
                            rhs = xn_own[:, dch * TOWN + tc * 512: dch * TOWN + (tc + 1) * 512]
                        else:
                            src = xn_own if tc < 2 else xn_part
                            tcc = tc % 2
                            rhs = src[:, dch * TOWN + tcc * 512: dch * TOWN + (tcc + 1) * 512]
                        vm = MM(banks[tc][:], lhsT, rhs, first, True,
                                inc=(c4 == 3 and tc == ntc - 1))
                mm_hist[g] = vm
            W("scalar", "s_pe", mm_hist[P.note["wg"] - 1])
            if si >= 2:
                W("scalar", f"s_wk{si % 2}", kv_wr[si - 2])
            ev = kvev[si % 2]
            bias = bqc[:, o0 // 128: o0 // 128 + 1]
            for tc in range(ntc):
                v_ev = SOP("activation", ev[:, tc * 512:(tc + 1) * 512], banks[tc][:], AF.Identity, bias=bias, inc=(tc == ntc - 1))
            evac_par[si % 2] = v_ev
            W("gpsimd", "s_ac", v_ev)
            if kind == "KV":
                kv_wr[si] = dma("gpsimd", kv_fm[o0 - D:o0 - D + 128, :], ev[:, 0:TALL], sem=f"s_wk{si % 2}")
            else:
                kv_wr[si] = dma("gpsimd", q_fm[o0:o0 + 128, :], ev[:, 0:TOWN], sem=f"s_wk{si % 2}")
        v_qkv_wk = (P.val("s_wk0"), P.val("s_wk1"))
        v_p1_evac = max(evac_par[0], evac_par[1])
        p1_cv = P.val("s_cv")
        p1_cp = P.val("s_cp")

        if "kv_dump" in tapsh:
            W("gpsimd", "s_wk0", v_qkv_wk[0])
            W("gpsimd", "s_wk1", v_qkv_wk[1])
            dma("gpsimd", tapsh["kv_dump"], kv_fm.bitcast(F32), sem="s_st")
        if "q_dump" in tapsh:
            W("gpsimd", "s_wk0", v_qkv_wk[0])
            W("gpsimd", "s_wk1", v_qkv_wk[1])
            dma("gpsimd", tapsh["q_dump"], q_fm.bitcast(F32), sem="s_st")

        W("tensor", "s_ac", v_p1_evac)
        for half in range(2):
            for dch in range(NCH):
                v_z = MM(ps[half][0:32, :], A_bf[:, dch * 32:(dch + 1) * 32],
                         xn_own[:, dch * TOWN + half * 512: dch * TOWN + (half + 1) * 512],
                         dch == 0, True, inc=(dch == NCH - 1))
        W("vector", "s_pe", v_z)
        VOP("tensor_copy", z_own[:, 0:512], ps[0][0:32, :])
        v_zc = VOP("tensor_copy", z_own[:, 512:1024], ps[1][0:32, :], inc=True)

        v_zero = VOP("memset", sc_acc[:], 0.0, inc=True)
        P.note["anch_prev"] = v_zero

        def anch_block(rhs_list, n):
            W("tensor", "s_cv", P.note["anch_prev"])
            for dch in range(NCH):
                vH = MM(ps[2][0:8, 0:n], w8_bf[:, dch * 8:(dch + 1) * 8], rhs_list[dch], dch == 0, True, inc=(dch == NCH - 1))
            for dch in range(NCH):
                vZ = MM(ps[3][0:32, 0:n], A_bf[:, dch * 32:(dch + 1) * 32], rhs_list[dch], dch == 0, True, inc=(dch == NCH - 1))
            W("vector", "s_pe", vZ)
            v_zb = VOP("tensor_copy", zfm_bf[0:32, 0:n], ps[3][0:32, 0:n], inc=True)
            W("tensor", "s_cv", v_zb)
            vD = MM(ps[4][0:8, 0:n], LiT_bf[:], zfm_bf[0:32, 0:n], True, True, inc=True)
            W("vector", "s_pe", vH)
            VC("tensor_scalar", hscr[0:8, 0:n], ps[2][0:8, 0:n], bfc[0:8, 0:1], None, ALU.add)
            VC("tensor_reduce", ga[0:8, 20:21], hscr[0:8, 0:n], axis=AX.X, op=ALU.max, apply_absolute_value=True)
            VC("tensor_max", sc_acc[0:8, 0:1], sc_acc[0:8, 0:1], ga[0:8, 20:21])
            W("vector", "s_pe", vD)
            VC("tensor_reduce", ga[0:8, 21:22], ps[4][0:8, 0:n], axis=AX.X, op=ALU.max, apply_absolute_value=True)
            v_sc = VC("tensor_max", sc_acc[0:8, 1:2], sc_acc[0:8, 1:2], ga[0:8, 21:22])
            P.note["anch_prev"] = v_sc
            return v_sc

        for blk in range(4):
            src = (xn_own, xn_part)[blk // 2]
            hf = blk % 2
            rl = [src[:, dch * TOWN + hf * 512 + 1: dch * TOWN + (hf + 1) * 512: 2] for dch in range(NCH)]
            anch_block(rl, 256)

        W("scalar", "s_wk0", v_qkv_wk[0])
        W("scalar", "s_wk1", v_qkv_wk[1])
        W("vector", "s_wk0", v_qkv_wk[0])
        W("vector", "s_wk1", v_qkv_wk[1])
        for ti in range(24):
            buf = xstage[0]
            tag = "xa%d" % (ti % 2)
            if P.note.get("xstage_war") is not None:
                W("sync", "s_ac", P.note["xstage_war"])
            v_x = dma("sync", buf[:], xa[ti * 128:(ti + 1) * 128, :])
            W("vector", "s_in", v_x)
            VOP("reduce_sum", ga[:, 0:1], buf[:], axis=AX.X)
            W("scalar", "s_in", v_x)
            pw = P.note.get("xn_tm_pewar")
            if pw:
                W("scalar", "s_pe", pw)
            for pc in range(8):
                SOP("activation", ps[5][:, 0:512], buf[:, pc * 512:(pc + 1) * 512], AF.Square,
                    accum_out=ga[:, 64 + pc:65 + pc], inc=(pc == 7))
            v_sq = P.val("s_ac")
            W("vector", "s_ac", v_sq)
            VC("reduce_sum", ga[:, 1:2], ga[:, 64:72], axis=AX.X)
            VC("tensor_scalar_mul", ga[:, 2:3], ga[:, 0:1], 1.0 / D)
            VC("tensor_scalar_mul", ga[:, 3:4], ga[:, 1:2], 1.0 / D)
            VC("tensor_mul", ga[:, 4:5], ga[:, 2:3], ga[:, 2:3])
            VC("tensor_sub", ga[:, 5:6], ga[:, 3:4], ga[:, 4:5])
            v_var = VC("tensor_scalar_add", ga[:, 5:6], ga[:, 5:6], 1e-5)
            W("scalar", "s_cv", v_var)
            v_rt = SOP("activation", ga[:, 6:7], ga[:, 5:6], AF.Sqrt, inc=True)
            W("vector", "s_ac", v_rt)
            VC("reciprocal", ga[:, 7:8], ga[:, 6:7])
            VC("tensor_mul", ga[:, 8:9], ga[:, 2:3], ga[:, 7:8])
            v_nmr = VC("tensor_scalar_mul", ga[:, 9:10], ga[:, 8:9], -1.0)
            W("scalar", "s_cv", v_nmr)
            v_norm = SOP("activation", xn_tm[:], buf[:], AF.Identity, bias=ga[:, 9:10], scale=ga[:, 7:8], inc=True)
            P.note["xstage_war"] = v_norm
            last_tp = None
            for grp in range(8):
                bank = ps[6 + (grp % 2)]
                W("tensor", "s_ac", v_norm)
                war = P.note.get("xatp_%d" % (grp % 2))
                if war:
                    W("tensor", "s_cv", war)
                for k4 in range(4):
                    ch = grp * 4 + k4
                    vv = MM(bank[:, k4 * 128:(k4 + 1) * 128], xn_tm[:, ch * 128:(ch + 1) * 128], ib_sb[:], True, True, inc=(k4 == 3))
                last_tp = vv
                W("vector", "s_pe", vv)

                def _ev2(V, bank=bank, grp=grp):
                    for k4 in range(4):
                        ch = grp * 4 + k4
                        dstb = kvev[ch // 16]
                        ins = V.tensor_copy(dstb[:, (ch % 16) * 128:((ch % 16) + 1) * 128], bank[:, k4 * 128:(k4 + 1) * 128])
                        if k4 == 3:
                            ins.then_inc(s_cv, 1)
                vev = P.bump("s_cv")
                P.emit("vector", _ev2)
                P.note["xatp_%d" % (grp % 2)] = vev
            P.note["xn_tm_pewar"] = last_tp
            W("tensor", "s_cv", vev)
            rl = [kvev[dch // 16][:, (dch % 16) * 128:((dch % 16) + 1) * 128] for dch in range(NCH)]
            anch_block(rl, 128)

        v_last = P.note["anch_prev"]
        W("tensor", "s_cv", v_last)
        MM(ps[2][0:1, 0:8], sc_acc[0:8, 0:1], if_sb[0:8, 0:8], True, True)
        v_sct = MM(ps[2][0:1, 8:16], sc_acc[0:8, 1:2], if_sb[0:8, 0:8], True, True, inc=True)
        W("vector", "s_pe", v_sct)
        VC("tensor_reduce", sc_fin[0:1, 0:1], ps[2][0:1, 0:8], axis=AX.X, op=ALU.max)
        VC("tensor_reduce", sc_fin[0:1, 1:2], ps[2][0:1, 8:16], axis=AX.X, op=ALU.max)
        VC("tensor_scalar_max", sc_fin[0:1, 0:2], sc_fin[0:1, 0:2], 1e-6)
        VC("reciprocal", sc_fin[0:1, 2:4], sc_fin[0:1, 0:2])
        v_scf = VC("tensor_scalar_mul", sc_fin[0:1, 4:6], sc_fin[0:1, 2:4], 0.5)
        if "scales" in tapsh:
            W("gpsimd", "s_cv", v_scf)
            dma("gpsimd", tapsh["scales"], sc_fin[0:1, 0:16], sem="s_st")
        W("gpsimd", "s_cv", v_scf)
        v_b1 = dma("gpsimd", sc_bounce, sc_fin[0:1, 4:6], sem="s_wr")
        W("gpsimd", "s_wr", v_b1)
        v_scbc = dma("gpsimd", sc_bc[:], sc_bounce.to_broadcast((128, 2)), sem="s_wr")

        gate_v = {}
        for a in range(4):
            rl = [xn_own[:, dch * TOWN + a * 256 + 1: dch * TOWN + (a + 1) * 256: 2] for dch in range(NCH)]
            if a > 0:
                W("tensor", "s_cv", gate_v[a - 1])
            for dch in range(NCH):
                vH = MM(ps[3][:, 0:8], rl[dch], w8_bf[:, dch * 8:(dch + 1) * 8], dch == 0, True, inc=(dch == NCH - 1))
            for dch in range(NCH):
                vZ = MM(ps[4][:, 0:32], rl[dch], A_bf[:, dch * 32:(dch + 1) * 32], dch == 0, True, inc=(dch == NCH - 1))
            W("vector", "s_pe", vZ)
            v_zt = VOP("tensor_copy", pt_sb[0][:, 0:32], ps[4][:, 0:32], inc=True)
            W("tensor", "s_cv", v_zt)
            v_zf = MM(ps[5][0:32, 0:128], pt_sb[0][:, 0:32], ib_sb[:], True, True, inc=True)
            W("vector", "s_pe", v_zf)
            v_zfc = VOP("tensor_copy", zfm_bf[0:32, 0:128], ps[5][0:32, 0:128], inc=True)
            W("tensor", "s_cv", v_zfc)
            v_dt = MM(ps[5][:, 0:8], zfm_bf[0:32, 0:128], LiT_bf[:], True, True, inc=True)
            W("vector", "s_pe", vH)
            W("vector", "s_pe", v_dt)
            W("vector", "s_wr", v_scbc)
            VC("tensor_add", ga[:, 16:24], ps[3][:, 0:8], b8bc[:])
            VC("tensor_scalar", ga[:, 16:24], ga[:, 16:24], sc_bc[:, 0:1], None, ALU.mult)
            VC("tensor_scalar", ga[:, 24:32], ps[5][:, 0:8], sc_bc[:, 1:2], None, ALU.mult)
            VC("tensor_add", ga[:, 16:24], ga[:, 16:24], ga[:, 24:32])
            v_lg = VC("tensor_reduce", ga[:, 32:33], ga[:, 16:24], axis=AX.X, op=ALU.max, negate=True)
            W("scalar", "s_cv", v_lg)
            v_ex = SOP("activation", ga[:, 33:41], ga[:, 16:24], AF.Exp, bias=ga[:, 32:33], accum_out=ga[:, 41:42], inc=True)
            W("vector", "s_ac", v_ex)
            VC("reciprocal", ga[:, 42:43], ga[:, 41:42])
            VC("tensor_scalar", ga[:, 33:41], ga[:, 33:41], ga[:, 42:43], None, ALU.mult)
            VC("memset", ga[:, 72:80], -1e30)
            VC("tensor_reduce", ga[:, 72:73], ga[:, 33:41], axis=AX.X, op=ALU.max)
            VC("match_replace", out=ga[:, 44:52], in_to_replace=ga[:, 72:80], in_values=ga[:, 33:41], imm_value=-1.0)
            VC("tensor_reduce", ga[:, 52:53], ga[:, 44:52], axis=AX.X, op=ALU.max)
            v_th = VC("tensor_scalar_mul", ga[:, 53:54], ga[:, 52:53], -2.0)
            W("scalar", "s_cv", v_th)
            v_sg = SOP("activation", ga[:, 54:62], ga[:, 33:41], AF.Sigmoid, bias=ga[:, 53:54], scale=2.0, inc=True)
            W("vector", "s_ac", v_sg)
            VC("tensor_mul", ga[:, 54:62], ga[:, 33:41], ga[:, 54:62])
            VC("reduce_sum", ga[:, 62:63], ga[:, 54:62], axis=AX.X)
            VC("tensor_scalar_add", ga[:, 62:63], ga[:, 62:63], 1e-9)
            VC("reciprocal", ga[:, 63:64], ga[:, 62:63])
            gate_v[a] = VC("tensor_scalar", gw[:, a * 8:(a + 1) * 8], ga[:, 54:62], ga[:, 63:64], None, ALU.mult)
        wc_v = {}
        for a in range(4):
            W("tensor", "s_cv", gate_v[a])
            if a > 0:
                W("tensor", "s_cv", wc_v[a - 1])
            v_wt = MM(ps[6][0:8, 0:128], gw[:, a * 8:(a + 1) * 8], if_sb[:], True, True, inc=True)
            W("vector", "s_pe", v_wt)
            wc_v[a] = VOP("tensor_copy", wf_fm[0:8, a * 128:(a + 1) * 128], ps[6][0:8, 0:128], inc=True)
        VOP("tensor_copy", wf_bf[0:8, 0:TOWN:2], wf_fm[0:8, 0:512])
        v_wfb = VOP("tensor_copy", wf_bf[0:8, 1:TOWN:2], wf_fm[0:8, 0:512], inc=True)
        if "wfull" in tapsh:
            W("gpsimd", "s_cv", v_wfb)
            dma("gpsimd", tapsh["wfull"], wf_fm[0:8, 0:512], sem="s_st")
        W("tensor", "s_cv", v_wfb)
        W("tensor", "s_cv", v_csm)
        zp_v = {}
        for h2 in range(2):
            if h2 > 0:
                W("tensor", "s_cv", zp_v[0])
            v_pm = MM(ps[7][0:32, :], Lis_bf[:], wf_bf[0:8, h2 * 512:(h2 + 1) * 512], True, True, inc=True)
            W("vector", "s_pe", v_pm)
            W("vector", "s_pe", v_z)
            zp_v[h2] = VOP("tensor_mul", zp_bf[0:32, h2 * 512:(h2 + 1) * 512], z_own[0:32, h2 * 512:(h2 + 1) * 512], ps[7][0:32, :], inc=True)
        v_zp = zp_v[1]
        p2_pe = P.val("s_pe")
        if "z_own" in tapsh:
            W("gpsimd", "s_cv", v_zp)
            dma("gpsimd", tapsh["z_own"], z_own[:], sem="s_st")
            dma("gpsimd", tapsh["zp"], zp_bf[:].bitcast(F32), sem="s_st")

        exp_v = {}
        ell_v = {}
        av_v = {}
        shf_v = {}
        atno = {}
        for h in range(NH):
            if h > 0:
                W("gpsimd", "s_pe", ell_v[(h - 1, 15)])
                W("gpsimd", "s_cv", atno[h - 1])
            else:
                W("gpsimd", "s_wk0", v_qkv_wk[0])
                W("gpsimd", "s_wk1", v_qkv_wk[1])
                W("gpsimd", "s_pe", p2_pe)
            v_kh = dma("gpsimd", kh, kv_fm[h * 128:(h + 1) * 128, :], sem="s_wr")
            v_vh = dma("gpsimd", vh, kv_fm[D + h * 128: D + (h + 1) * 128, :], sem="s_wr")
            v_qh = dma("gpsimd", qh[:], q_fm[h * 128:(h + 1) * 128, :], sem="s_wr")
            v_cb = dma("gpsimd", cb_sb, ctab[h:h + 1, :].to_broadcast((128, TOWN)), sem="s_wr")
            W("tensor", "s_wr", v_cb)
            if h > 0:
                W("tensor", "s_cv", atno[h - 1])
            else:
                W("tensor", "s_cv", v_zp)
            vt_ev = {}
            for l in range(16):
                bank = ps[4 + (l % 2)]
                if l >= 2:
                    W("tensor", "s_cv", vt_ev[l - 2])
                v_tp = MM(bank[:, 0:128], vh[:, l * 128:(l + 1) * 128], ib_sb[:], True, True, inc=True)
                W("vector", "s_pe", v_tp)
                vt_ev[l] = VOP("tensor_copy", vt[:, l * 128:(l + 1) * 128], bank[:, 0:128], inc=True)
            v_vt = vt_ev[15]
            step = 0
            for i in range(8):
                for which in range(2):
                    l = i if which == 0 else 8 + i
                    q0 = i * 128
                    Nq = TOWN - q0
                    stb = (ps[0 + 2 * (step % 2)], ps[1 + 2 * (step % 2)])
                    ptb = pt_sb[step % 2]
                    W("tensor", "s_wr", v_cb)
                    if step >= 2:
                        W("tensor", "s_cv", shf_v[(h, step - 2)])
                    elif h > 0:
                        W("tensor", "s_cv", shf_v[(h - 1, 14 + step)])
                    nhalf = (Nq + 511) // 512
                    for hf in range(nhalf):
                        n = min(512, Nq - hf * 512)
                        v_st = MM(stb[hf][:, 0:n], kh[:, l * 128:(l + 1) * 128], qh[:, q0 + hf * 512: q0 + hf * 512 + n], True, True, inc=(hf == nhalf - 1))
                    W("vector", "s_pe", v_st)
                    W("vector", "s_wr", v_cb)
                    if step >= 1:
                        W("vector", "s_ac", exp_v[(h, step - 1)])
                    elif h > 0:
                        W("vector", "s_ac", exp_v[(h - 1, 15)])
                    else:
                        W("vector", "s_pe", p2_pe)
                    for hf in range(nhalf):
                        n = min(512, Nq - hf * 512)
                        VOP("scalar_tensor_tensor", st_scr[:, hf * 512:hf * 512 + n], stb[hf][:, 0:n], SCALE,
                            cb_sb[:, q0 + hf * 512: q0 + hf * 512 + n], ALU.mult, ALU.add)
                    msk = tri_sb if which == 0 else pm_sb
                    v_shf = VOP("tensor_add", st_scr[:, 0:128], st_scr[:, 0:128], msk[:], inc=True)
                    shf_v[(h, step)] = v_shf
                    W("scalar", "s_cv", v_shf)
                    if step >= 2:
                        W("scalar", "s_pe", ell_v[(h, step - 2)])
                    elif h > 0:
                        W("scalar", "s_pe", ell_v[(h - 1, 14 + step)])
                    acol = alibi_sb[:, h * 16 + l: h * 16 + l + 1]
                    for hf in range(nhalf):
                        n = min(512, Nq - hf * 512)
                        v_exp = SOP("activation", ptb[:, hf * 512:hf * 512 + n], st_scr[:, hf * 512:hf * 512 + n], AF.Exp,
                                    bias=acol, inc=(hf == nhalf - 1))
                    exp_v[(h, step)] = v_exp
                    W("tensor", "s_ac", v_exp)
                    if step == 0:
                        W("tensor", "s_cv", v_vt)
                    first = step == 0
                    pieces = []
                    if q0 < 512:
                        pieces.append((0, q0, 512 - q0, 0))
                        pieces.append((1, 0, 512, 512 - q0))
                    else:
                        pieces.append((1, q0 - 512, 1024 - q0, 0))
                    for pi, (bi, boff, n, po) in enumerate(pieces):
                        v_av = MM(ps[4 + bi][:, boff:boff + n], vt[:, l * 128:(l + 1) * 128], ptb[:, po:po + n], first, True, inc=(pi == len(pieces) - 1))
                    av_v[(h, step)] = v_av
                    for pi, (bi, boff, n, po) in enumerate(pieces):
                        v_el = MM(ps[6 + bi][0:1, boff:boff + n], ones_sb[:], ptb[:, po:po + n], first, True, inc=(pi == len(pieces) - 1))
                    ell_v[(h, step)] = v_el
                    step += 1
            W("vector", "s_pe", ell_v[(h, 15)])
            W("vector", "s_pe", av_v[(h, 15)])
            VOP("reciprocal", ellinv[0:1, 0:512], ps[6][0:1, :])
            v_ei = VOP("reciprocal", ellinv[0:1, 512:1024], ps[7][0:1, :], inc=True)
            W("gpsimd", "s_cv", v_ei)
            v_rb = dma("gpsimd", el_bounce, ellinv[0:1, :], sem="s_wr")
            W("gpsimd", "s_wr", v_rb)
            v_bc = dma("gpsimd", linv_bc[:], el_bounce.to_broadcast((128, TOWN)), sem="s_wr")
            W("vector", "s_wr", v_bc)
            if h == 0:
                W("vector", "s_pe", p2_pe)
            VOP("tensor_mul", att_fm[:, h * TOWN: h * TOWN + 512], ps[4][:], linv_bc[:, 0:512])
            atno[h] = VOP("tensor_mul", att_fm[:, h * TOWN + 512: (h + 1) * TOWN], ps[5][:], linv_bc[:, 512:1024], inc=True)
        v_att = atno[NH - 1]
        if "attn_out" in tapsh:
            W("gpsimd", "s_cv", v_att)
            dma("gpsimd", tapsh["attn_out"], att_fm[:, 0:NH * TOWN].bitcast(F32), sem="s_st")

        W("sync", "s_cv", p1_cv)
        W("sync", "s_cp", p1_cp)
        fmm = {}
        evH_prev = 0
        evS_prev = 0
        sg_v = {}
        finadd = {}
        outst = {}
        fin_ev = {}
        tail_pend = []

        def emit_tail(sl):
            o0 = sl * 128
            if sl > 0:
                W("gpsimd", "s_cv", finadd[sl - 1])
            else:
                W("gpsimd", "s_cv", v_att)
            for tt in range(8):
                v_xr = dma("gpsimd", xr_sb[:, tt * 128:(tt + 1) * 128], x_loc[tt * 128:(tt + 1) * 128, o0:o0 + 128], sem="s_wr")
            P.note["xr_%d" % sl] = v_xr
            for tt in range(8):
                bank = ps[4 + (tt % 2)]
                W("tensor", "s_cv", sg_v[sl])
                war = fin_ev.get(sl * 8 + tt - 2)
                if sl == 0 and tt < 2:
                    W("tensor", "s_cv", v_att)
                if war:
                    W("tensor", "s_cv", war)
                v_tp = MM(bank[:, 0:128], g_sb[:, tt * 128:(tt + 1) * 128], if_sb[:], True, True, inc=True)
                W("vector", "s_pe", v_tp)
                W("vector", "s_wr", P.note["xr_%d" % sl])
                pst = outst.get(sl * 8 + tt - 1)
                if pst:
                    W("vector", "s_st", pst)
                v_ad = VOP("tensor_add", o_sb[:, tt * 128:(tt + 1) * 128], xr_sb[:, tt * 128:(tt + 1) * 128], bank[:, 0:128], inc=True)
                fin_ev[sl * 8 + tt] = v_ad
                finadd[sl] = v_ad
                W("gpsimd", "s_cv", v_ad)
                outst[sl * 8 + tt] = dma("gpsimd", out_loc[tt * 128:(tt + 1) * 128, o0:o0 + 128], o_sb[:, tt * 128:(tt + 1) * 128], sem="s_st")

        for sl in range(32):
            o0 = sl * 128
            for grp in range(8):
                for wsel in range(2):
                    src = wf_pk[sl, grp] if wsel == 0 else wo_pk[sl, grp]
                    g, wi, csem, v_c = wgroup_dma_cast(src)
                    W("tensor", csem, v_c)
                    if grp == 0:
                        if wsel == 0:
                            W("tensor", "s_ac", evH_prev)
                            if sl > 0:
                                W("tensor", "s_cv", sg_v[sl - 1])
                        else:
                            W("tensor", "s_ac", evS_prev)
                            if sl == 0:
                                W("tensor", "s_cv", v_att)
                    pb = (ps[0], ps[1]) if wsel == 0 else (ps[2], ps[3])
                    rhs0 = xn_own if wsel == 0 else att_fm
                    for c4 in range(4):
                        dch = grp * 4 + c4
                        first = dch == 0
                        lhsT = wgbf[wi][:, c4 * 128:(c4 + 1) * 128]
                        for hf in range(2):
                            rhs = rhs0[:, dch * TOWN + hf * 512: dch * TOWN + (hf + 1) * 512]
                            vm = MM(pb[hf][:], lhsT, rhs, first, True, inc=(c4 == 3 and hf == 1))
                    fmm[(sl, grp, wsel)] = vm
                    mm_hist[g] = vm
            if sl > 0:
                emit_tail(sl - 1)
            bmi = sl % 2
            if sl >= 2:
                W("sync", "s_cv", P.note["bmcast_%d" % (sl - 2)])
            v_bm = dma("sync", bm_st[bmi][:], Bmat[:, o0:o0 + 128], sem=f"s_bm{bmi}")
            W("vector", f"s_bm{bmi}", v_bm)
            if sl >= 2:
                W("vector", "s_pe", P.note["bmuse_%d" % (sl - 2)])
            v_bmc = VOP("tensor_copy", bm_bf[bmi][:], bm_st[bmi][:], inc=True)
            P.note["bmcast_%d" % sl] = v_bmc
            W("tensor", "s_cv", v_bmc)
            if sl == 0:
                W("tensor", "s_cv", v_zp)
            for hf in range(2):
                v_dl = MM(ps[hf][:], bm_bf[bmi][0:32, :], zp_bf[0:32, hf * 512:(hf + 1) * 512], False, True, inc=(hf == 1))
            P.note["bmuse_%d" % sl] = v_dl
            W("scalar", "s_pe", v_dl)
            if sl > 0:
                W("scalar", "s_cv", sg_v[sl - 1])
            bcol = bfc[:, sl:sl + 1]
            SOP("activation", g_sb[:, 0:512], ps[0][:], AF.Silu, bias=bcol)
            evH_prev = SOP("activation", g_sb[:, 512:1024], ps[1][:], AF.Silu, bias=bcol, inc=True)
            W("scalar", "s_pe", fmm[(sl, 7, 1)])
            ocol = boc[:, sl:sl + 1]
            SOP("activation", s_sb[:, 0:512], ps[2][:], AF.Identity, bias=ocol)
            evS_prev = SOP("activation", s_sb[:, 512:1024], ps[3][:], AF.Identity, bias=ocol, inc=True)
            W("vector", "s_ac", evS_prev)
            sg_v[sl] = VOP("tensor_mul", g_sb[:], g_sb[:], s_sb[:], inc=True)
        emit_tail(31)
        W("gpsimd", "s_st", P.val("s_st"))

        block = e(nc.Block())

        @block.sync
        def _(E):
            for fn in P.q["sync"]:
                fn(E)

        @block.vector
        def _(E):
            for fn in P.q["vector"]:
                fn(E)

        @block.scalar
        def _(E):
            for fn in P.q["scalar"]:
                fn(E)

        @block.tensor
        def _(E):
            for fn in P.q["tensor"]:
                fn(E)

        @block.gpsimd
        def _(E):
            for fn in P.q["gpsimd"]:
                fn(E)

    return nc



def _alibi_slopes():
    m = np.arange(1, NH + 1, dtype=np.float64) * (8.0 / NH)
    return 1.0 / np.power(2.0, m)


def _core_inputs(c, x, shared):
    b, j = c // 2, c % 2
    blocks = x[b].reshape(16, 128, D)
    x_loc = np.ascontiguousarray(
        np.concatenate([blocks[j::2].reshape(TOWN, D), blocks[1 - j::2].reshape(TOWN, D)], axis=0))
    others = [q for q in range(B) if q != b]
    xav = np.ascontiguousarray(x[others][:, 1::2, :].reshape(3072, D))
    slopes = _alibi_slopes()
    ac = np.zeros((128, 512), dtype=np.float32)
    p = np.arange(128)
    for h in range(NH):
        for l in range(16):
            g = 2 * l + j if l < 8 else 2 * (l - 8) + (1 - j)
            ac[:, h * 16 + l] = (slopes[h] * (g * 128 + p - (S - 1))).astype(np.float32)
    pm = (np.full((128, 128), -1e9, np.float32) if j == 0 else np.zeros((128, 128), np.float32))
    q = np.arange(TOWN)
    gpos = (2 * (q // 128) + j) * 128 + (q % 128)
    ct = (slopes[:, None] * ((S - 1) - gpos)[None, :]).astype(np.float32)
    d = {"x_loc": x_loc, "xa": xav, "alibi_cols": ac, "pm_tile": pm, "ctab": ct}
    d.update(shared)
    return d


def _pack_wT(wT):
    O = wT.shape[1]
    a = wT.reshape(8, 4, 128, O // 128, 128)
    return np.ascontiguousarray(a.transpose(3, 0, 2, 1, 4).reshape(O // 128, 8, 128, 512))


def make_in_maps(L0_attn_x, L0_attn_w_qkv, L0_attn_b_qkv, L0_attn_w_out, L0_attn_b_out,
                 L0_ffn_W, L0_ffn_b, L0_ffn_A, L0_ffn_Bmat, L0_ffn_LiMEs):
    x = np.asarray(L0_attn_x, dtype=np.float32)
    LiMEs = np.asarray(L0_ffn_LiMEs, np.float32)
    w_qkvT = np.asarray(L0_attn_w_qkv, np.float32).T
    W_Tm = np.asarray(L0_ffn_W, np.float32).T
    shared = {
        "wq_pk": _pack_wT(w_qkvT),
        "b_qkv": np.asarray(L0_attn_b_qkv, np.float32),
        "wo_pk": _pack_wT(np.asarray(L0_attn_w_out, np.float32).T),
        "b_out": np.asarray(L0_attn_b_out, np.float32),
        "wf_pk": _pack_wT(W_Tm),
        "W_T8": np.ascontiguousarray(W_Tm[:, 0:8]),
        "b_ffn": np.asarray(L0_ffn_b, np.float32),
        "A_in": np.asarray(L0_ffn_A, np.float32),
        "Bmat": np.asarray(L0_ffn_Bmat, np.float32),
        "LiT": np.ascontiguousarray(LiMEs.T),
        "Li": LiMEs,
    }
    return [_core_inputs(c, x, shared) for c in range(8)]


def assemble(results):
    out = np.zeros((B, S, D), dtype=np.float32)
    for c in range(8):
        b, j = c // 2, c % 2
        r = np.asarray(results[c]["out_loc"]).reshape(8, 128, D)
        for i in range(8):
            g = 2 * i + j
            out[b, g * 128:(g + 1) * 128, :] = r[i]
    return out


def kernel(L0_attn_x, L0_attn_w_qkv, L0_attn_b_qkv, L0_attn_w_out, L0_attn_b_out,
           L0_ffn_W, L0_ffn_b, L0_ffn_A, L0_ffn_Bmat, L0_ffn_LiMEs):
    in_maps = make_in_maps(L0_attn_x, L0_attn_w_qkv, L0_attn_b_qkv, L0_attn_w_out,
                           L0_attn_b_out, L0_ffn_W, L0_ffn_b, L0_ffn_A, L0_ffn_Bmat,
                           L0_ffn_LiMEs)
    nc = build()
    res = run_bass_kernel_spmd(nc, in_maps, core_ids=list(range(8)))
    return assemble(res.results)


# version 12
# speedup vs baseline: 1.1642x; 1.1642x over previous
import numpy as np
import ml_dtypes
import concourse.bass as bass
import concourse.mybir as mybir
from concourse.bass_utils import run_bass_kernel_spmd
from contextlib import ExitStack

F32 = mybir.dt.float32
BF16 = mybir.dt.bfloat16
AF = mybir.ActivationFunctionType
ALU = mybir.AluOpType
AX = mybir.AxisListType
BF = ml_dtypes.bfloat16

D = 4096
S = 2048
B = 4
NH = 32
TOWN = 1024
TALL = 2048
NCH = 32
SCALE = float(1.0 / np.sqrt(128.0))


class Prog:

    def __init__(self):
        self.q = {"sync": [], "vector": [], "scalar": [], "tensor": [], "gpsimd": []}
        self.cnt = {}
        self.note = {}

    def bump(self, sem, delta=1):
        self.cnt[sem] = self.cnt.get(sem, 0) + delta
        return self.cnt[sem]

    def val(self, sem):
        return self.cnt.get(sem, 0)

    def emit(self, eng, fn):
        self.q[eng].append(fn)


def build(taps=()):
    nc = bass.Bass("TRN2", target_bir_lowering=False, debug=False)

    x_loc = nc.dram_tensor("x_loc", [TALL, D], F32, kind="ExternalInput").ap()
    xa = nc.dram_tensor("xa", [3072, D], F32, kind="ExternalInput").ap()
    wq_pk = nc.dram_tensor("wq_pk", [96, 8, 128, 512], F32, kind="ExternalInput").ap()
    b_qkv = nc.dram_tensor("b_qkv", [3 * D], F32, kind="ExternalInput").ap()
    wo_pk = nc.dram_tensor("wo_pk", [32, 8, 128, 512], F32, kind="ExternalInput").ap()
    b_out = nc.dram_tensor("b_out", [D], F32, kind="ExternalInput").ap()
    wf_pk = nc.dram_tensor("wf_pk", [32, 8, 128, 512], F32, kind="ExternalInput").ap()
    W_T8 = nc.dram_tensor("W_T8", [D, 8], F32, kind="ExternalInput").ap()
    b_ffn = nc.dram_tensor("b_ffn", [D], F32, kind="ExternalInput").ap()
    A_in = nc.dram_tensor("A_in", [D, 32], F32, kind="ExternalInput").ap()
    Bmat = nc.dram_tensor("Bmat", [32, D], F32, kind="ExternalInput").ap()
    LiT = nc.dram_tensor("LiT", [32, 8], F32, kind="ExternalInput").ap()
    Li = nc.dram_tensor("Li", [8, 32], F32, kind="ExternalInput").ap()
    alibi_cols = nc.dram_tensor("alibi_cols", [128, 512], F32, kind="ExternalInput").ap()
    ctab = nc.dram_tensor("ctab", [32, TOWN], F32, kind="ExternalInput").ap()
    pm_tile = nc.dram_tensor("pm_tile", [128, 128], F32, kind="ExternalInput").ap()

    out_loc = nc.dram_tensor("out_loc", [TOWN, D], F32, kind="ExternalOutput").ap()

    ident_b = nc.inline_tensor(np.eye(128, dtype=BF), name="ident_b").ap()
    ident_f = nc.inline_tensor(np.eye(128, dtype=np.float32), name="ident_f").ap()
    ones_bi = nc.inline_tensor(np.ones((128, 1), dtype=BF), name="ones_bi").ap()
    tri_np = np.where(np.arange(128)[:, None] <= np.arange(128)[None, :], 0.0, -1e9).astype(np.float32)
    tri_in = nc.inline_tensor(tri_np, name="tri_in").ap()

    kv_fm = nc.dram_tensor("kv_fm", [2 * D, TALL], BF16, kind="Internal").ap()
    q_fm = nc.dram_tensor("q_fm", [D, TOWN], BF16, kind="Internal").ap()
    sc_bounce = nc.dram_tensor("sc_bounce", [1, 2], F32, kind="Internal").ap()
    el_bounce = nc.dram_tensor("el_bounce", [1, TOWN], F32, kind="Internal").ap()

    tapsh = {}
    tapspec = {"xn_own": ([128, NCH * TOWN // 2], "bf"), "xn_part": ([128, NCH * TOWN // 2], "bf"),
               "kv_dump": ([2 * D, TALL // 2], "bf"), "q_dump": ([D, TOWN // 2], "bf"),
               "attn_out": ([128, NH * TOWN // 2], "bf"), "scales": ([1, 16], "f"),
               "wfull": ([8, 512], "f"), "z_own": ([32, TOWN], "f"),
               "zp": ([32, TOWN // 2], "bf")}
    tap_is_bf = {}
    for t in taps:
        sh, dt = tapspec[t]
        tap_is_bf[t] = dt == "bf"
        tapsh[t] = nc.dram_tensor("tap_" + t, sh, F32, kind="ExternalOutput").ap()

    P = Prog()

    with ExitStack() as stx:
        e = stx.enter_context

        xn_own = e(nc.sbuf_tensor("xn_own", [128, NCH * TOWN], BF16))
        xn_part = e(nc.sbuf_tensor("xn_part", [128, NCH * TOWN], BF16))
        xstage = [e(nc.sbuf_tensor("xstage0", [128, D], F32))]
        xn_tm = e(nc.sbuf_tensor("xn_tm", [128, D], BF16))
        wgst = [e(nc.sbuf_tensor(f"wgst{i}", [128, 512], F32)) for i in range(2)]
        wgbf = [e(nc.sbuf_tensor(f"wgbf{i}", [128, 512], BF16)) for i in range(2)]
        kvev = [e(nc.sbuf_tensor(f"kvev{i}", [128, TALL], BF16)) for i in range(2)]
        bqc = e(nc.sbuf_tensor("bqc", [128, 96], F32))
        bfc = e(nc.sbuf_tensor("bfc", [128, 32], F32))
        boc = e(nc.sbuf_tensor("boc", [128, 32], F32))
        b8bc = e(nc.sbuf_tensor("b8bc", [128, 8], F32))
        btmp = e(nc.sbuf_tensor("btmp", [96, 128], F32))
        A_bf = e(nc.sbuf_tensor("A_bf", [128, NCH * 32], BF16))
        bm_st = [e(nc.sbuf_tensor(f"bm_st{i}", [32, 128], F32)) for i in range(2)]
        bm_bf = [e(nc.sbuf_tensor(f"bm_bf{i}", [32, 128], BF16)) for i in range(2)]
        Lis_bf = e(nc.sbuf_tensor("Lis_bf", [8, 32], BF16))
        LiT_bf = e(nc.sbuf_tensor("LiT_bf", [32, 8], BF16))
        w8_bf = e(nc.sbuf_tensor("w8_bf", [128, NCH * 8], BF16))
        ib_sb = e(nc.sbuf_tensor("ib_sb", [128, 128], BF16))
        if_sb = e(nc.sbuf_tensor("if_sb", [128, 128], F32))
        ones_sb = e(nc.sbuf_tensor("ones_sb", [128, 1], BF16))
        tri_sb = e(nc.sbuf_tensor("tri_sb", [128, 128], F32))
        pm_sb = e(nc.sbuf_tensor("pm_sb", [128, 128], F32))
        alibi_sb = e(nc.sbuf_tensor("alibi_sb", [128, 512], F32))
        kh = xn_tm[:, 0:TALL]
        vh = xn_tm[:, TALL:2 * TALL]
        vt = vh
        qh = e(nc.sbuf_tensor("qh", [128, TOWN], BF16))
        pt_sb = [e(nc.sbuf_tensor(f"pt{i}", [128, TOWN], BF16)) for i in range(2)]
        zfm_bf = e(nc.sbuf_tensor("zfm_bf", [32, 512], BF16))
        linv_bc = e(nc.sbuf_tensor("linv_bc", [128, TOWN], F32))
        ellinv = linv_bc
        att_fm = xn_part
        z_own = e(nc.sbuf_tensor("z_own", [32, TOWN], F32))
        zp_bf = e(nc.sbuf_tensor("zp_bf", [32, TOWN], BF16))
        ga = e(nc.sbuf_tensor("ga", [128, 80], F32))
        gw = e(nc.sbuf_tensor("gw", [128, 32], F32))
        wf_fm = e(nc.sbuf_tensor("wf_fm", [8, 512], F32))
        wf_bf = e(nc.sbuf_tensor("wf_bf", [8, TOWN], BF16))
        sc_acc = e(nc.sbuf_tensor("sc_acc", [8, 4], F32))
        sc_fin = e(nc.sbuf_tensor("sc_fin", [1, 16], F32))
        sc_bc = e(nc.sbuf_tensor("sc_bc", [128, 2], F32))
        hscr = wf_fm
        g_sb = kvev[0][:].bitcast(F32)
        s_sb = kvev[1][:].bitcast(F32)
        st_scr0 = kvev[0][:].bitcast(F32)
        cb_sb = kvev[1][:].bitcast(F32)
        xr_sb = linv_bc
        o_sb = e(nc.sbuf_tensor("o_sb", [128, TOWN], F32))

        ps = [e(nc.psum_tensor(f"ps{i}", [128, 512], F32)) for i in range(8)]

        s_in = e(nc.semaphore("s_in"))
        s_wr = e(nc.semaphore("s_wr"))
        s_w = [e(nc.semaphore(f"s_w{i}")) for i in range(4)]
        s_w2 = [e(nc.semaphore(f"s_w2{i}")) for i in range(4)]
        s_wk = [e(nc.semaphore(f"s_wk{i}")) for i in range(2)]
        s_bt = e(nc.semaphore("s_bt"))
        s_at = e(nc.semaphore("s_at"))
        s_bm = [e(nc.semaphore(f"s_bm{i}")) for i in range(2)]
        s_cv = e(nc.semaphore("s_cv"))
        s_cp = e(nc.semaphore("s_cp"))
        s_pe = e(nc.semaphore("s_pe"))
        s_ac = e(nc.semaphore("s_ac"))
        s_st = e(nc.semaphore("s_st"))
        SEM = {"s_in": s_in, "s_wr": s_wr, "s_cv": s_cv, "s_cp": s_cp,
               "s_pe": s_pe, "s_ac": s_ac, "s_st": s_st}
        for i in range(4):
            SEM[f"s_w{i}"] = s_w[i]
            SEM[f"s_w2{i}"] = s_w2[i]
        for i in range(2):
            SEM[f"s_wk{i}"] = s_wk[i]
            SEM[f"s_bm{i}"] = s_bm[i]
        SEM["s_bt"] = s_bt
        SEM["s_at"] = s_at

        def W(eng, sem, v):
            if v > 0:
                P.emit(eng, lambda E, S_=SEM[sem], v=v: E.wait_ge(S_, v))

        def dma(eng, dst, src, sem="s_in"):
            v = P.bump(sem, 16)
            P.emit(eng, lambda E, d=dst, s=src, S_=SEM[sem]: E.dma_start(out=d, in_=s).then_inc(S_, 16))
            return v

        def MM(out, lhsT, rhs, start, stop, inc=False):
            if inc:
                v = P.bump("s_pe")
                P.emit("tensor", lambda T, o=out, l=lhsT, r=rhs, a=start, b=stop: T.matmul(o, l, r, start=a, stop=b).then_inc(s_pe, 1))
                return v
            P.emit("tensor", lambda T, o=out, l=lhsT, r=rhs, a=start, b=stop: T.matmul(o, l, r, start=a, stop=b))
            return None

        def VOP(opname, *args, inc=None, **kw):
            if inc:
                v = P.bump("s_cv")
                P.emit("vector", lambda V, o=opname, args=args, kw=kw: getattr(V, o)(*args, **kw).then_inc(s_cv, 1))
                return v
            P.emit("vector", lambda V, o=opname, args=args, kw=kw: getattr(V, o)(*args, **kw))
            return None

        def VC(opname, *args, **kw):
            v = VOP(opname, *args, inc=True, **kw)
            W("vector", "s_cv", v)
            return v

        def SOP(opname, *args, inc=None, **kw):
            if inc:
                v = P.bump("s_ac")
                P.emit("scalar", lambda Sc, o=opname, args=args, kw=kw: getattr(Sc, o)(*args, **kw).then_inc(s_ac, 1))
                return v
            P.emit("scalar", lambda Sc, o=opname, args=args, kw=kw: getattr(Sc, o)(*args, **kw))
            return None

        dma("sync", ib_sb[:], ident_b)
        dma("sync", if_sb[:], ident_f)
        dma("sync", ones_sb[:], ones_bi)
        dma("sync", alibi_sb[:], alibi_cols)
        dma("sync", tri_sb[:], tri_in)
        dma("sync", pm_sb[:], pm_tile)
        dma("sync", xstage[0][:, 0:1024].rearrange("p (c r) -> p c r", c=32), A_in.rearrange("(c p) r -> p c r", p=128))
        dma("sync", xstage[0][0:8, 1024:1056], Li)
        dma("sync", xstage[0][0:32, 1056:1064], LiT)
        dma("sync", xstage[0][:, 1088:1344].rearrange("p (c e) -> p c e", c=32), W_T8.rearrange("(c p) e -> p c e", p=128))
        dma("gpsimd", b8bc[:], b_ffn[0:8].rearrange("(o e) -> o e", o=1).to_broadcast((128, 8)), sem="s_wr")
        v_smalls = P.val("s_in")
        W("vector", "s_in", v_smalls)
        VOP("tensor_copy", A_bf[:], xstage[0][:, 0:1024])
        VOP("tensor_copy", LiT_bf[:], xstage[0][0:32, 1056:1064])
        VOP("tensor_copy", w8_bf[:], xstage[0][:, 1088:1344])
        v_csm = VOP("tensor_scalar_mul", Lis_bf[:], xstage[0][0:8, 1024:1056], 0.5, inc=True)

        v_bq = dma("sync", btmp[0:96, :], b_qkv.rearrange("(c p) -> c p", p=128), sem="s_bt")
        W("tensor", "s_bt", v_bq)
        v_t1 = MM(ps[0][:, 0:96], btmp[0:96, :], if_sb[0:96, 0:96], True, True, inc=True)
        W("vector", "s_pe", v_t1)
        v_c1 = VOP("tensor_copy", bqc[:], ps[0][:, 0:96], inc=True)
        W("sync", "s_cv", v_c1)
        v_bf2 = dma("sync", btmp[0:32, :], b_ffn.rearrange("(c p) -> c p", p=128), sem="s_bt")
        W("tensor", "s_bt", v_bf2)
        W("tensor", "s_cv", v_c1)
        v_t2 = MM(ps[0][:, 0:32], btmp[0:32, :], if_sb[0:32, 0:32], True, True, inc=True)
        W("vector", "s_pe", v_t2)
        v_c2 = VOP("tensor_copy", bfc[:], ps[0][:, 0:32], inc=True)
        W("sync", "s_cv", v_c2)
        v_bo = dma("sync", btmp[0:32, :], b_out.rearrange("(c p) -> c p", p=128), sem="s_bt")
        W("tensor", "s_bt", v_bo)
        W("tensor", "s_cv", v_c2)
        v_t3 = MM(ps[0][:, 0:32], btmp[0:32, :], if_sb[0:32, 0:32], True, True, inc=True)
        W("vector", "s_pe", v_t3)
        v_c3 = VOP("tensor_copy", boc[:], ps[0][:, 0:32], inc=True)

        def ln_tile(buf, src_rows_ap, tag, dst, dcol, tp_banks):
            if P.note.get("xstage_war") is not None:
                W("sync", "s_ac", P.note["xstage_war"])
            v_x = dma("sync", buf[:], src_rows_ap)
            W("vector", "s_in", v_x)
            v_sum = VOP("reduce_sum", ga[:, 0:1], buf[:], axis=AX.X, inc=True)
            W("scalar", "s_in", v_x)
            pe_war = P.note.get("xn_tm_pewar")
            if pe_war:
                W("scalar", "s_pe", pe_war)
            for pc in range(8):
                SOP("activation", ps[5][:, 0:512], buf[:, pc * 512:(pc + 1) * 512], AF.Square,
                    accum_out=ga[:, 64 + pc:65 + pc], inc=(pc == 7))
            v_sq = P.val("s_ac")
            W("vector", "s_ac", v_sq)
            VC("reduce_sum", ga[:, 1:2], ga[:, 64:72], axis=AX.X)
            VC("tensor_scalar_mul", ga[:, 2:3], ga[:, 0:1], 1.0 / D)
            VC("tensor_scalar_mul", ga[:, 3:4], ga[:, 1:2], 1.0 / D)
            VC("tensor_mul", ga[:, 4:5], ga[:, 2:3], ga[:, 2:3])
            VC("tensor_sub", ga[:, 5:6], ga[:, 3:4], ga[:, 4:5])
            v_var = VC("tensor_scalar_add", ga[:, 5:6], ga[:, 5:6], 1e-5)
            W("scalar", "s_cv", v_var)
            v_rt = SOP("activation", ga[:, 6:7], ga[:, 5:6], AF.Sqrt, inc=True)
            W("vector", "s_ac", v_rt)
            VC("reciprocal", ga[:, 7:8], ga[:, 6:7])
            VC("tensor_mul", ga[:, 8:9], ga[:, 2:3], ga[:, 7:8])
            v_nmr = VC("tensor_scalar_mul", ga[:, 9:10], ga[:, 8:9], -1.0)
            W("scalar", "s_cv", v_nmr)
            v_norm = SOP("activation", xn_tm[:], buf[:], AF.Identity, bias=ga[:, 9:10], scale=ga[:, 7:8], inc=True)
            P.note["xstage_war"] = v_norm
            last_tp = None
            for grp in range(8):
                bank = tp_banks[grp % 2]
                W("tensor", "s_ac", v_norm)
                war = P.note.get("tpev_" + str(tp_banks[0].name if hasattr(tp_banks[0], 'name') else 0) + "_" + str(grp % 2))
                if war:
                    W("tensor", "s_cv", war)
                for k4 in range(4):
                    ch = grp * 4 + k4
                    vv = MM(bank[:, k4 * 128:(k4 + 1) * 128], xn_tm[:, ch * 128:(ch + 1) * 128], ib_sb[:], True, True, inc=(k4 == 3))
                last_tp = vv
                W("vector", "s_pe", vv)

                def _ev(V, bank=bank, grp=grp, dst=dst, dcol=dcol):
                    for k4 in range(4):
                        ch = grp * 4 + k4
                        ins = V.tensor_copy(dst[:, ch * TOWN + dcol: ch * TOWN + dcol + 128], bank[:, k4 * 128:(k4 + 1) * 128])
                        if k4 == 3:
                            ins.then_inc(s_cv, 1)
                vev = P.bump("s_cv")
                P.emit("vector", _ev)
                P.note["tpev_" + str(tp_banks[0].name if hasattr(tp_banks[0], 'name') else 0) + "_" + str(grp % 2)] = vev
            P.note["xn_tm_pewar"] = last_tp
            return v_norm

        W("sync", "s_cv", v_csm)
        for ti in range(16):
            dst = xn_own if ti < 8 else xn_part
            ln_tile(xstage[0], x_loc[ti * 128:(ti + 1) * 128, :], "ln", 
                    dst, (ti % 8) * 128, (ps[6], ps[7]))
        v_xn_cv = P.val("s_cv")

        if "xn_own" in tapsh:
            W("gpsimd", "s_cv", v_xn_cv)
            dma("gpsimd", tapsh["xn_own"], xn_own[:].bitcast(F32), sem="s_st")
            dma("gpsimd", tapsh["xn_part"], xn_part[:].bitcast(F32), sem="s_st")

        slabs = [("KV", o) for o in range(D, 3 * D, 128)] + [("Q", o) for o in range(0, D, 128)]
        gch = 0
        cast_hist = {}
        mm_hist = {}
        evac_par = {0: 0, 1: 0}
        kv_wr = {}

        def wgroup_dma_cast(src_ap):
            g = P.note.get("wg", 0)
            P.note["wg"] = g + 1
            wi = g % 2
            if g >= 2:
                sems, vv = cast_hist[g - 2]
                W("sync", sems, vv)
            v_w = dma("sync", wgst[wi][:], src_ap, sem=f"s_w{wi}")
            ceng = "vector" if (g % 5) != 4 else "gpsimd"
            csem = "s_cv" if ceng == "vector" else "s_cp"
            W(ceng, f"s_w{wi}", v_w)
            if g >= 2:
                W(ceng, "s_pe", mm_hist[g - 2])
            v_c = P.bump(csem)
            cast_hist[g] = (csem, v_c)
            P.emit(ceng, lambda E, a=wgbf[wi], b=wgst[wi], S_=SEM[csem]: E.tensor_copy(a[:], b[:]).then_inc(S_, 1))
            return g, wi, csem, v_c

        for si, (kind, o0) in enumerate(slabs):
            ntc = 4 if kind == "KV" else 2
            banks = [ps[(si % 2) * 4 + t] for t in range(ntc)]
            for grp in range(8):
                g, wi, csem, v_c = wgroup_dma_cast(wq_pk[o0 // 128, grp])
                W("tensor", csem, v_c)
                if si == 0 and grp == 0:
                    W("tensor", "s_cv", v_xn_cv)
                if grp == 0:
                    W("tensor", "s_ac", evac_par[si % 2])
                for c4 in range(4):
                    dch = grp * 4 + c4
                    first = dch == 0
                    lhsT = wgbf[wi][:, c4 * 128:(c4 + 1) * 128]
                    for tc in range(ntc):
                        if kind == "Q":
                            rhs = xn_own[:, dch * TOWN + tc * 512: dch * TOWN + (tc + 1) * 512]
                        else:
                            src = xn_own if tc < 2 else xn_part
                            tcc = tc % 2
                            rhs = src[:, dch * TOWN + tcc * 512: dch * TOWN + (tcc + 1) * 512]
                        vm = MM(banks[tc][:], lhsT, rhs, first, True,
                                inc=(c4 == 3 and tc == ntc - 1))
                mm_hist[g] = vm
            W("scalar", "s_pe", mm_hist[P.note["wg"] - 1])
            if si >= 2:
                W("scalar", f"s_wk{si % 2}", kv_wr[si - 2])
            ev = kvev[si % 2]
            bias = bqc[:, o0 // 128: o0 // 128 + 1]
            for tc in range(ntc):
                v_ev = SOP("activation", ev[:, tc * 512:(tc + 1) * 512], banks[tc][:], AF.Identity, bias=bias, inc=(tc == ntc - 1))
            evac_par[si % 2] = v_ev
            W("gpsimd", "s_ac", v_ev)
            if kind == "KV":
                kv_wr[si] = dma("gpsimd", kv_fm[o0 - D:o0 - D + 128, :], ev[:, 0:TALL], sem=f"s_wk{si % 2}")
            else:
                kv_wr[si] = dma("gpsimd", q_fm[o0:o0 + 128, :], ev[:, 0:TOWN], sem=f"s_wk{si % 2}")
        v_qkv_wk = (P.val("s_wk0"), P.val("s_wk1"))
        v_p1_evac = max(evac_par[0], evac_par[1])
        p1_cv = P.val("s_cv")
        p1_cp = P.val("s_cp")

        if "kv_dump" in tapsh:
            W("gpsimd", "s_wk0", v_qkv_wk[0])
            W("gpsimd", "s_wk1", v_qkv_wk[1])
            dma("gpsimd", tapsh["kv_dump"], kv_fm.bitcast(F32), sem="s_st")
        if "q_dump" in tapsh:
            W("gpsimd", "s_wk0", v_qkv_wk[0])
            W("gpsimd", "s_wk1", v_qkv_wk[1])
            dma("gpsimd", tapsh["q_dump"], q_fm.bitcast(F32), sem="s_st")

        W("tensor", "s_ac", v_p1_evac)
        for half in range(2):
            for dch in range(NCH):
                v_z = MM(ps[half][0:32, :], A_bf[:, dch * 32:(dch + 1) * 32],
                         xn_own[:, dch * TOWN + half * 512: dch * TOWN + (half + 1) * 512],
                         dch == 0, True, inc=(dch == NCH - 1))
        W("vector", "s_pe", v_z)
        VOP("tensor_copy", z_own[:, 0:512], ps[0][0:32, :])
        v_zc = VOP("tensor_copy", z_own[:, 512:1024], ps[1][0:32, :], inc=True)

        v_zero = VOP("memset", sc_acc[:], 0.0, inc=True)
        P.note["anch_prev"] = v_zero

        def anch_block(rhs_list, n):
            W("tensor", "s_cv", P.note["anch_prev"])
            for dch in range(NCH):
                vH = MM(ps[2][0:8, 0:n], w8_bf[:, dch * 8:(dch + 1) * 8], rhs_list[dch], dch == 0, True, inc=(dch == NCH - 1))
            for dch in range(NCH):
                vZ = MM(ps[3][0:32, 0:n], A_bf[:, dch * 32:(dch + 1) * 32], rhs_list[dch], dch == 0, True, inc=(dch == NCH - 1))
            W("vector", "s_pe", vZ)
            v_zb = VOP("tensor_copy", zfm_bf[0:32, 0:n], ps[3][0:32, 0:n], inc=True)
            W("tensor", "s_cv", v_zb)
            vD = MM(ps[4][0:8, 0:n], LiT_bf[:], zfm_bf[0:32, 0:n], True, True, inc=True)
            W("vector", "s_pe", vH)
            VC("tensor_scalar", hscr[0:8, 0:n], ps[2][0:8, 0:n], bfc[0:8, 0:1], None, ALU.add)
            VC("tensor_reduce", ga[0:8, 20:21], hscr[0:8, 0:n], axis=AX.X, op=ALU.max, apply_absolute_value=True)
            VC("tensor_max", sc_acc[0:8, 0:1], sc_acc[0:8, 0:1], ga[0:8, 20:21])
            W("vector", "s_pe", vD)
            VC("tensor_reduce", ga[0:8, 21:22], ps[4][0:8, 0:n], axis=AX.X, op=ALU.max, apply_absolute_value=True)
            v_sc = VC("tensor_max", sc_acc[0:8, 1:2], sc_acc[0:8, 1:2], ga[0:8, 21:22])
            P.note["anch_prev"] = v_sc
            return v_sc

        for blk in range(4):
            src = (xn_own, xn_part)[blk // 2]
            hf = blk % 2
            rl = [src[:, dch * TOWN + hf * 512 + 1: dch * TOWN + (hf + 1) * 512: 2] for dch in range(NCH)]
            anch_block(rl, 256)

        W("scalar", "s_wk0", v_qkv_wk[0])
        W("scalar", "s_wk1", v_qkv_wk[1])
        W("vector", "s_wk0", v_qkv_wk[0])
        W("vector", "s_wk1", v_qkv_wk[1])
        for ti in range(24):
            buf = xstage[0]
            tag = "xa%d" % (ti % 2)
            if P.note.get("xstage_war") is not None:
                W("sync", "s_ac", P.note["xstage_war"])
            v_x = dma("sync", buf[:], xa[ti * 128:(ti + 1) * 128, :])
            W("vector", "s_in", v_x)
            VOP("reduce_sum", ga[:, 0:1], buf[:], axis=AX.X)
            W("scalar", "s_in", v_x)
            pw = P.note.get("xn_tm_pewar")
            if pw:
                W("scalar", "s_pe", pw)
            for pc in range(8):
                SOP("activation", ps[5][:, 0:512], buf[:, pc * 512:(pc + 1) * 512], AF.Square,
                    accum_out=ga[:, 64 + pc:65 + pc], inc=(pc == 7))
            v_sq = P.val("s_ac")
            W("vector", "s_ac", v_sq)
            VC("reduce_sum", ga[:, 1:2], ga[:, 64:72], axis=AX.X)
            VC("tensor_scalar_mul", ga[:, 2:3], ga[:, 0:1], 1.0 / D)
            VC("tensor_scalar_mul", ga[:, 3:4], ga[:, 1:2], 1.0 / D)
            VC("tensor_mul", ga[:, 4:5], ga[:, 2:3], ga[:, 2:3])
            VC("tensor_sub", ga[:, 5:6], ga[:, 3:4], ga[:, 4:5])
            v_var = VC("tensor_scalar_add", ga[:, 5:6], ga[:, 5:6], 1e-5)
            W("scalar", "s_cv", v_var)
            v_rt = SOP("activation", ga[:, 6:7], ga[:, 5:6], AF.Sqrt, inc=True)
            W("vector", "s_ac", v_rt)
            VC("reciprocal", ga[:, 7:8], ga[:, 6:7])
            VC("tensor_mul", ga[:, 8:9], ga[:, 2:3], ga[:, 7:8])
            v_nmr = VC("tensor_scalar_mul", ga[:, 9:10], ga[:, 8:9], -1.0)
            W("scalar", "s_cv", v_nmr)
            v_norm = SOP("activation", xn_tm[:], buf[:], AF.Identity, bias=ga[:, 9:10], scale=ga[:, 7:8], inc=True)
            P.note["xstage_war"] = v_norm
            last_tp = None
            for grp in range(8):
                bank = ps[6 + (grp % 2)]
                W("tensor", "s_ac", v_norm)
                war = P.note.get("xatp_%d" % (grp % 2))
                if war:
                    W("tensor", "s_cv", war)
                for k4 in range(4):
                    ch = grp * 4 + k4
                    vv = MM(bank[:, k4 * 128:(k4 + 1) * 128], xn_tm[:, ch * 128:(ch + 1) * 128], ib_sb[:], True, True, inc=(k4 == 3))
                last_tp = vv
                W("vector", "s_pe", vv)

                def _ev2(V, bank=bank, grp=grp):
                    for k4 in range(4):
                        ch = grp * 4 + k4
                        dstb = kvev[ch // 16]
                        ins = V.tensor_copy(dstb[:, (ch % 16) * 128:((ch % 16) + 1) * 128], bank[:, k4 * 128:(k4 + 1) * 128])
                        if k4 == 3:
                            ins.then_inc(s_cv, 1)
                vev = P.bump("s_cv")
                P.emit("vector", _ev2)
                P.note["xatp_%d" % (grp % 2)] = vev
            P.note["xn_tm_pewar"] = last_tp
            W("tensor", "s_cv", vev)
            rl = [kvev[dch // 16][:, (dch % 16) * 128:((dch % 16) + 1) * 128] for dch in range(NCH)]
            anch_block(rl, 128)

        v_last = P.note["anch_prev"]
        W("tensor", "s_cv", v_last)
        MM(ps[2][0:1, 0:8], sc_acc[0:8, 0:1], if_sb[0:8, 0:8], True, True)
        v_sct = MM(ps[2][0:1, 8:16], sc_acc[0:8, 1:2], if_sb[0:8, 0:8], True, True, inc=True)
        W("vector", "s_pe", v_sct)
        VC("tensor_reduce", sc_fin[0:1, 0:1], ps[2][0:1, 0:8], axis=AX.X, op=ALU.max)
        VC("tensor_reduce", sc_fin[0:1, 1:2], ps[2][0:1, 8:16], axis=AX.X, op=ALU.max)
        VC("tensor_scalar_max", sc_fin[0:1, 0:2], sc_fin[0:1, 0:2], 1e-6)
        VC("reciprocal", sc_fin[0:1, 2:4], sc_fin[0:1, 0:2])
        v_scf = VC("tensor_scalar_mul", sc_fin[0:1, 4:6], sc_fin[0:1, 2:4], 0.5)
        if "scales" in tapsh:
            W("gpsimd", "s_cv", v_scf)
            dma("gpsimd", tapsh["scales"], sc_fin[0:1, 0:16], sem="s_st")
        W("gpsimd", "s_cv", v_scf)
        v_b1 = dma("gpsimd", sc_bounce, sc_fin[0:1, 4:6], sem="s_wr")
        W("gpsimd", "s_wr", v_b1)
        v_scbc = dma("gpsimd", sc_bc[:], sc_bounce.to_broadcast((128, 2)), sem="s_wr")

        gate_v = {}
        for a in range(4):
            rl = [xn_own[:, dch * TOWN + a * 256 + 1: dch * TOWN + (a + 1) * 256: 2] for dch in range(NCH)]
            if a > 0:
                W("tensor", "s_cv", gate_v[a - 1])
            for dch in range(NCH):
                vH = MM(ps[3][:, 0:8], rl[dch], w8_bf[:, dch * 8:(dch + 1) * 8], dch == 0, True, inc=(dch == NCH - 1))
            for dch in range(NCH):
                vZ = MM(ps[4][:, 0:32], rl[dch], A_bf[:, dch * 32:(dch + 1) * 32], dch == 0, True, inc=(dch == NCH - 1))
            W("vector", "s_pe", vZ)
            v_zt = VOP("tensor_copy", pt_sb[0][:, 0:32], ps[4][:, 0:32], inc=True)
            W("tensor", "s_cv", v_zt)
            v_zf = MM(ps[5][0:32, 0:128], pt_sb[0][:, 0:32], ib_sb[:], True, True, inc=True)
            W("vector", "s_pe", v_zf)
            v_zfc = VOP("tensor_copy", zfm_bf[0:32, 0:128], ps[5][0:32, 0:128], inc=True)
            W("tensor", "s_cv", v_zfc)
            v_dt = MM(ps[5][:, 0:8], zfm_bf[0:32, 0:128], LiT_bf[:], True, True, inc=True)
            W("vector", "s_pe", vH)
            W("vector", "s_pe", v_dt)
            W("vector", "s_wr", v_scbc)
            VC("tensor_add", ga[:, 16:24], ps[3][:, 0:8], b8bc[:])
            VC("tensor_scalar", ga[:, 16:24], ga[:, 16:24], sc_bc[:, 0:1], None, ALU.mult)
            VC("tensor_scalar", ga[:, 24:32], ps[5][:, 0:8], sc_bc[:, 1:2], None, ALU.mult)
            VC("tensor_add", ga[:, 16:24], ga[:, 16:24], ga[:, 24:32])
            v_lg = VC("tensor_reduce", ga[:, 32:33], ga[:, 16:24], axis=AX.X, op=ALU.max, negate=True)
            W("scalar", "s_cv", v_lg)
            v_ex = SOP("activation", ga[:, 33:41], ga[:, 16:24], AF.Exp, bias=ga[:, 32:33], accum_out=ga[:, 41:42], inc=True)
            W("vector", "s_ac", v_ex)
            VC("reciprocal", ga[:, 42:43], ga[:, 41:42])
            VC("tensor_scalar", ga[:, 33:41], ga[:, 33:41], ga[:, 42:43], None, ALU.mult)
            VC("memset", ga[:, 72:80], -1e30)
            VC("tensor_reduce", ga[:, 72:73], ga[:, 33:41], axis=AX.X, op=ALU.max)
            VC("match_replace", out=ga[:, 44:52], in_to_replace=ga[:, 72:80], in_values=ga[:, 33:41], imm_value=-1.0)
            VC("tensor_reduce", ga[:, 52:53], ga[:, 44:52], axis=AX.X, op=ALU.max)
            v_th = VC("tensor_scalar_mul", ga[:, 53:54], ga[:, 52:53], -2.0)
            W("scalar", "s_cv", v_th)
            v_sg = SOP("activation", ga[:, 54:62], ga[:, 33:41], AF.Sigmoid, bias=ga[:, 53:54], scale=2.0, inc=True)
            W("vector", "s_ac", v_sg)
            VC("tensor_mul", ga[:, 54:62], ga[:, 33:41], ga[:, 54:62])
            VC("reduce_sum", ga[:, 62:63], ga[:, 54:62], axis=AX.X)
            VC("tensor_scalar_add", ga[:, 62:63], ga[:, 62:63], 1e-9)
            VC("reciprocal", ga[:, 63:64], ga[:, 62:63])
            gate_v[a] = VC("tensor_scalar", gw[:, a * 8:(a + 1) * 8], ga[:, 54:62], ga[:, 63:64], None, ALU.mult)
        wc_v = {}
        for a in range(4):
            W("tensor", "s_cv", gate_v[a])
            if a > 0:
                W("tensor", "s_cv", wc_v[a - 1])
            v_wt = MM(ps[6][0:8, 0:128], gw[:, a * 8:(a + 1) * 8], if_sb[:], True, True, inc=True)
            W("vector", "s_pe", v_wt)
            wc_v[a] = VOP("tensor_copy", wf_fm[0:8, a * 128:(a + 1) * 128], ps[6][0:8, 0:128], inc=True)
        VOP("tensor_copy", wf_bf[0:8, 0:TOWN:2], wf_fm[0:8, 0:512])
        v_wfb = VOP("tensor_copy", wf_bf[0:8, 1:TOWN:2], wf_fm[0:8, 0:512], inc=True)
        if "wfull" in tapsh:
            W("gpsimd", "s_cv", v_wfb)
            dma("gpsimd", tapsh["wfull"], wf_fm[0:8, 0:512], sem="s_st")
        W("tensor", "s_cv", v_wfb)
        W("tensor", "s_cv", v_csm)
        zp_v = {}
        for h2 in range(2):
            if h2 > 0:
                W("tensor", "s_cv", zp_v[0])
            v_pm = MM(ps[7][0:32, :], Lis_bf[:], wf_bf[0:8, h2 * 512:(h2 + 1) * 512], True, True, inc=True)
            W("vector", "s_pe", v_pm)
            W("vector", "s_pe", v_z)
            zp_v[h2] = VOP("tensor_mul", zp_bf[0:32, h2 * 512:(h2 + 1) * 512], z_own[0:32, h2 * 512:(h2 + 1) * 512], ps[7][0:32, :], inc=True)
        v_zp = zp_v[1]
        p2_pe = P.val("s_pe")
        if "z_own" in tapsh:
            W("gpsimd", "s_cv", v_zp)
            dma("gpsimd", tapsh["z_own"], z_own[:], sem="s_st")
            dma("gpsimd", tapsh["zp"], zp_bf[:].bitcast(F32), sem="s_st")

        exp_v = {}
        ell_v = {}
        av_v = {}
        shf_v = {}
        atno = {}
        for h in range(NH):
            if h > 0:
                W("gpsimd", "s_pe", ell_v[(h - 1, 15)])
            else:
                W("gpsimd", "s_wk0", v_qkv_wk[0])
                W("gpsimd", "s_wk1", v_qkv_wk[1])
                W("gpsimd", "s_pe", p2_pe)
            v_kh = dma("gpsimd", kh, kv_fm[h * 128:(h + 1) * 128, :], sem="s_at")
            v_vh = dma("gpsimd", vh, kv_fm[D + h * 128: D + (h + 1) * 128, :], sem="s_at")
            v_qh = dma("gpsimd", qh[:], q_fm[h * 128:(h + 1) * 128, :], sem="s_at")
            v_cb = dma("gpsimd", cb_sb, ctab[h:h + 1, :].to_broadcast((128, TOWN)), sem="s_at")
            def emit_vtrans(h=h):
                W("tensor", "s_at", v_cb)
                if h > 0:
                    W("tensor", "s_cv", atno[h - 1])
                else:
                    W("tensor", "s_cv", v_zp)
                vt_ev = {}
                for l in range(16):
                    bank = ps[4 + (l % 2)]
                    if l >= 2:
                        W("tensor", "s_cv", vt_ev[l - 2])
                    v_tp = MM(bank[:, 0:128], vh[:, l * 128:(l + 1) * 128], ib_sb[:], True, True, inc=True)
                    W("vector", "s_pe", v_tp)
                    vt_ev[l] = VOP("tensor_copy", vt[:, l * 128:(l + 1) * 128], bank[:, 0:128], inc=True)
                return vt_ev[15]
            v_vt = None
            step = 0
            pend = None

            def emit_av(p):
                (l_, q0_, Nq_, ptb_, first_, vexp_) = p
                W("tensor", "s_ac", vexp_)
                if first_:
                    W("tensor", "s_cv", v_vt)
                pieces = []
                if q0_ < 512:
                    pieces.append((0, q0_, 512 - q0_, 0))
                    pieces.append((1, 0, 512, 512 - q0_))
                else:
                    pieces.append((1, q0_ - 512, 1024 - q0_, 0))
                for pi, (bi, boff, n, po) in enumerate(pieces):
                    v_av = MM(ps[4 + bi][:, boff:boff + n], vt[:, l_ * 128:(l_ + 1) * 128], ptb_[:, po:po + n], first_, True, inc=(pi == len(pieces) - 1))
                for pi, (bi, boff, n, po) in enumerate(pieces):
                    v_el = MM(ps[6 + bi][0:1, boff:boff + n], ones_sb[:], ptb_[:, po:po + n], first_, True, inc=(pi == len(pieces) - 1))
                return v_av, v_el

            for i in range(8):
                for which in range(2):
                    l = i if which == 0 else 8 + i
                    q0 = i * 128
                    Nq = TOWN - q0
                    stb = (ps[0 + 2 * (step % 2)], ps[1 + 2 * (step % 2)])
                    ptb = pt_sb[step % 2]
                    W("tensor", "s_at", v_cb)
                    if step >= 2:
                        W("tensor", "s_cv", shf_v[(h, step - 2)])
                        shp_war = P.note.get("shp_%d_%d" % (h, step - 2))
                        if shp_war:
                            W("tensor", "s_cp", shp_war)
                    elif h > 0:
                        W("tensor", "s_cv", shf_v[(h - 1, 14 + step)])
                        shp_war = P.note.get("shp_%d_%d" % (h - 1, 14 + step))
                        if shp_war:
                            W("tensor", "s_cp", shp_war)
                    nhalf = (Nq + 511) // 512
                    for hf in range(nhalf):
                        n = min(512, Nq - hf * 512)
                        v_st = MM(stb[hf][:, 0:n], kh[:, l * 128:(l + 1) * 128], qh[:, q0 + hf * 512: q0 + hf * 512 + n], True, True, inc=(hf == nhalf - 1))
                    if pend is not None:
                        av_v[(h, step - 1)], ell_v[(h, step - 1)] = emit_av(pend)
                    st_scr = st_scr0 if step % 2 == 0 else o_sb[:]
                    W("vector", "s_pe", v_st)
                    W("vector", "s_at", v_cb)
                    if step >= 2:
                        W("vector", "s_ac", exp_v[(h, step - 2)])
                    elif h > 0:
                        W("vector", "s_ac", exp_v[(h - 1, 14 + step)])
                    elif step == 0:
                        W("vector", "s_pe", p2_pe)
                    for hf in range(nhalf):
                        n = min(512, Nq - hf * 512)
                        VOP("scalar_tensor_tensor", st_scr[:, hf * 512:hf * 512 + n], stb[hf][:, 0:n], SCALE,
                            cb_sb[:, q0 + hf * 512:q0 + hf * 512 + n], ALU.mult, ALU.add)
                    msk = tri_sb if which == 0 else pm_sb
                    v_shf = VOP("tensor_add", st_scr[:, 0:128], st_scr[:, 0:128], msk[:], inc=True)
                    shf_v[(h, step)] = v_shf
                    P.note["shp_%d_%d" % (h, step)] = None
                    W("scalar", "s_cv", v_shf)
                    if step >= 2:
                        W("scalar", "s_pe", ell_v.get((h, step - 2), 0))
                    elif h > 0:
                        W("scalar", "s_pe", ell_v[(h - 1, 14 + step)])
                    acol = alibi_sb[:, h * 16 + l: h * 16 + l + 1]
                    for hf in range(nhalf):
                        n = min(512, Nq - hf * 512)
                        v_exp = SOP("activation", ptb[:, hf * 512:hf * 512 + n], st_scr[:, hf * 512:hf * 512 + n], AF.Exp,
                                    bias=acol, inc=(hf == nhalf - 1))
                    exp_v[(h, step)] = v_exp
                    if step == 0:
                        v_vt = emit_vtrans()
                    pend = (l, q0, Nq, ptb, step == 0, v_exp)
                    step += 1
            av_v[(h, 15)], ell_v[(h, 15)] = emit_av(pend)
            W("vector", "s_pe", ell_v[(h, 15)])
            W("vector", "s_pe", av_v[(h, 15)])
            VOP("reciprocal", ellinv[0:1, 0:512], ps[6][0:1, :])
            v_ei = VOP("reciprocal", ellinv[0:1, 512:1024], ps[7][0:1, :], inc=True)
            W("gpsimd", "s_cv", v_ei)
            v_rb = dma("gpsimd", el_bounce, ellinv[0:1, :], sem="s_wr")
            W("gpsimd", "s_wr", v_rb)
            v_bc = dma("gpsimd", linv_bc[:], el_bounce.to_broadcast((128, TOWN)), sem="s_wr")
            W("vector", "s_wr", v_bc)
            if h == 0:
                W("vector", "s_pe", p2_pe)
            VOP("tensor_mul", att_fm[:, h * TOWN: h * TOWN + 512], ps[4][:], linv_bc[:, 0:512])
            atno[h] = VOP("tensor_mul", att_fm[:, h * TOWN + 512: (h + 1) * TOWN], ps[5][:], linv_bc[:, 512:1024], inc=True)
        v_att = atno[NH - 1]
        if "attn_out" in tapsh:
            W("gpsimd", "s_cv", v_att)
            dma("gpsimd", tapsh["attn_out"], att_fm[:, 0:NH * TOWN].bitcast(F32), sem="s_st")

        W("sync", "s_cv", p1_cv)
        W("sync", "s_cp", p1_cp)
        fmm = {}
        evH_prev = 0
        evS_prev = 0
        sg_v = {}
        finadd = {}
        outst = {}
        fin_ev = {}
        tail_pend = []

        def emit_tail(sl):
            o0 = sl * 128
            if sl > 0:
                W("gpsimd", "s_cv", finadd[sl - 1])
            else:
                W("gpsimd", "s_cv", v_att)
            for tt in range(8):
                v_xr = dma("gpsimd", xr_sb[:, tt * 128:(tt + 1) * 128], x_loc[tt * 128:(tt + 1) * 128, o0:o0 + 128], sem="s_wr")
            P.note["xr_%d" % sl] = v_xr
            for tt in range(8):
                bank = ps[4 + (tt % 2)]
                W("tensor", "s_cv", sg_v[sl])
                war = fin_ev.get(sl * 8 + tt - 2)
                if sl == 0 and tt < 2:
                    W("tensor", "s_cv", v_att)
                if war:
                    W("tensor", "s_cv", war)
                v_tp = MM(bank[:, 0:128], g_sb[:, tt * 128:(tt + 1) * 128], if_sb[:], True, True, inc=True)
                W("vector", "s_pe", v_tp)
                W("vector", "s_wr", P.note["xr_%d" % sl])
                pst = outst.get(sl * 8 + tt - 1)
                if pst:
                    W("vector", "s_st", pst)
                v_ad = VOP("tensor_add", o_sb[:, tt * 128:(tt + 1) * 128], xr_sb[:, tt * 128:(tt + 1) * 128], bank[:, 0:128], inc=True)
                fin_ev[sl * 8 + tt] = v_ad
                finadd[sl] = v_ad
                W("gpsimd", "s_cv", v_ad)
                outst[sl * 8 + tt] = dma("gpsimd", out_loc[tt * 128:(tt + 1) * 128, o0:o0 + 128], o_sb[:, tt * 128:(tt + 1) * 128], sem="s_st")

        for sl in range(32):
            o0 = sl * 128
            for grp in range(8):
                for wsel in range(2):
                    src = wf_pk[sl, grp] if wsel == 0 else wo_pk[sl, grp]
                    g, wi, csem, v_c = wgroup_dma_cast(src)
                    W("tensor", csem, v_c)
                    if grp == 0:
                        if wsel == 0:
                            W("tensor", "s_ac", evH_prev)
                            if sl > 0:
                                W("tensor", "s_cv", sg_v[sl - 1])
                        else:
                            W("tensor", "s_ac", evS_prev)
                            if sl == 0:
                                W("tensor", "s_cv", v_att)
                    pb = (ps[0], ps[1]) if wsel == 0 else (ps[2], ps[3])
                    rhs0 = xn_own if wsel == 0 else att_fm
                    for c4 in range(4):
                        dch = grp * 4 + c4
                        first = dch == 0
                        lhsT = wgbf[wi][:, c4 * 128:(c4 + 1) * 128]
                        for hf in range(2):
                            rhs = rhs0[:, dch * TOWN + hf * 512: dch * TOWN + (hf + 1) * 512]
                            vm = MM(pb[hf][:], lhsT, rhs, first, True, inc=(c4 == 3 and hf == 1))
                    fmm[(sl, grp, wsel)] = vm
                    mm_hist[g] = vm
            if sl > 0:
                emit_tail(sl - 1)
            bmi = sl % 2
            if sl >= 2:
                W("sync", "s_cv", P.note["bmcast_%d" % (sl - 2)])
            v_bm = dma("sync", bm_st[bmi][:], Bmat[:, o0:o0 + 128], sem=f"s_bm{bmi}")
            W("vector", f"s_bm{bmi}", v_bm)
            if sl >= 2:
                W("vector", "s_pe", P.note["bmuse_%d" % (sl - 2)])
            v_bmc = VOP("tensor_copy", bm_bf[bmi][:], bm_st[bmi][:], inc=True)
            P.note["bmcast_%d" % sl] = v_bmc
            W("tensor", "s_cv", v_bmc)
            if sl == 0:
                W("tensor", "s_cv", v_zp)
            for hf in range(2):
                v_dl = MM(ps[hf][:], bm_bf[bmi][0:32, :], zp_bf[0:32, hf * 512:(hf + 1) * 512], False, True, inc=(hf == 1))
            P.note["bmuse_%d" % sl] = v_dl
            W("scalar", "s_pe", v_dl)
            if sl > 0:
                W("scalar", "s_cv", sg_v[sl - 1])
            bcol = bfc[:, sl:sl + 1]
            SOP("activation", g_sb[:, 0:512], ps[0][:], AF.Silu, bias=bcol)
            evH_prev = SOP("activation", g_sb[:, 512:1024], ps[1][:], AF.Silu, bias=bcol, inc=True)
            W("scalar", "s_pe", fmm[(sl, 7, 1)])
            ocol = boc[:, sl:sl + 1]
            SOP("activation", s_sb[:, 0:512], ps[2][:], AF.Identity, bias=ocol)
            evS_prev = SOP("activation", s_sb[:, 512:1024], ps[3][:], AF.Identity, bias=ocol, inc=True)
            W("vector", "s_ac", evS_prev)
            sg_v[sl] = VOP("tensor_mul", g_sb[:], g_sb[:], s_sb[:], inc=True)
        emit_tail(31)
        W("gpsimd", "s_st", P.val("s_st"))

        block = e(nc.Block())

        @block.sync
        def _(E):
            for fn in P.q["sync"]:
                fn(E)

        @block.vector
        def _(E):
            for fn in P.q["vector"]:
                fn(E)

        @block.scalar
        def _(E):
            for fn in P.q["scalar"]:
                fn(E)

        @block.tensor
        def _(E):
            for fn in P.q["tensor"]:
                fn(E)

        @block.gpsimd
        def _(E):
            for fn in P.q["gpsimd"]:
                fn(E)

    return nc



def _alibi_slopes():
    m = np.arange(1, NH + 1, dtype=np.float64) * (8.0 / NH)
    return 1.0 / np.power(2.0, m)


def _core_inputs(c, x, shared):
    b, j = c // 2, c % 2
    blocks = x[b].reshape(16, 128, D)
    x_loc = np.ascontiguousarray(
        np.concatenate([blocks[j::2].reshape(TOWN, D), blocks[1 - j::2].reshape(TOWN, D)], axis=0))
    others = [q for q in range(B) if q != b]
    xav = np.ascontiguousarray(x[others][:, 1::2, :].reshape(3072, D))
    slopes = _alibi_slopes()
    ac = np.zeros((128, 512), dtype=np.float32)
    p = np.arange(128)
    for h in range(NH):
        for l in range(16):
            g = 2 * l + j if l < 8 else 2 * (l - 8) + (1 - j)
            ac[:, h * 16 + l] = (slopes[h] * (g * 128 + p - (S - 1))).astype(np.float32)
    pm = (np.full((128, 128), -1e9, np.float32) if j == 0 else np.zeros((128, 128), np.float32))
    q = np.arange(TOWN)
    gpos = (2 * (q // 128) + j) * 128 + (q % 128)
    ct = (slopes[:, None] * ((S - 1) - gpos)[None, :]).astype(np.float32)
    d = {"x_loc": x_loc, "xa": xav, "alibi_cols": ac, "pm_tile": pm, "ctab": ct}
    d.update(shared)
    return d


def _pack_wT(wT):
    O = wT.shape[1]
    a = wT.reshape(8, 4, 128, O // 128, 128)
    return np.ascontiguousarray(a.transpose(3, 0, 2, 1, 4).reshape(O // 128, 8, 128, 512))


def make_in_maps(L0_attn_x, L0_attn_w_qkv, L0_attn_b_qkv, L0_attn_w_out, L0_attn_b_out,
                 L0_ffn_W, L0_ffn_b, L0_ffn_A, L0_ffn_Bmat, L0_ffn_LiMEs):
    x = np.asarray(L0_attn_x, dtype=np.float32)
    LiMEs = np.asarray(L0_ffn_LiMEs, np.float32)
    w_qkvT = np.asarray(L0_attn_w_qkv, np.float32).T
    W_Tm = np.asarray(L0_ffn_W, np.float32).T
    shared = {
        "wq_pk": _pack_wT(w_qkvT),
        "b_qkv": np.asarray(L0_attn_b_qkv, np.float32),
        "wo_pk": _pack_wT(np.asarray(L0_attn_w_out, np.float32).T),
        "b_out": np.asarray(L0_attn_b_out, np.float32),
        "wf_pk": _pack_wT(W_Tm),
        "W_T8": np.ascontiguousarray(W_Tm[:, 0:8]),
        "b_ffn": np.asarray(L0_ffn_b, np.float32),
        "A_in": np.asarray(L0_ffn_A, np.float32),
        "Bmat": np.asarray(L0_ffn_Bmat, np.float32),
        "LiT": np.ascontiguousarray(LiMEs.T),
        "Li": LiMEs,
    }
    return [_core_inputs(c, x, shared) for c in range(8)]


def assemble(results):
    out = np.zeros((B, S, D), dtype=np.float32)
    for c in range(8):
        b, j = c // 2, c % 2
        r = np.asarray(results[c]["out_loc"]).reshape(8, 128, D)
        for i in range(8):
            g = 2 * i + j
            out[b, g * 128:(g + 1) * 128, :] = r[i]
    return out


def kernel(L0_attn_x, L0_attn_w_qkv, L0_attn_b_qkv, L0_attn_w_out, L0_attn_b_out,
           L0_ffn_W, L0_ffn_b, L0_ffn_A, L0_ffn_Bmat, L0_ffn_LiMEs):
    in_maps = make_in_maps(L0_attn_x, L0_attn_w_qkv, L0_attn_b_qkv, L0_attn_w_out,
                           L0_attn_b_out, L0_ffn_W, L0_ffn_b, L0_ffn_A, L0_ffn_Bmat,
                           L0_ffn_LiMEs)
    nc = build()
    res = run_bass_kernel_spmd(nc, in_maps, core_ids=list(range(8)))
    return assemble(res.results)
